# version 06
# baseline (speedup 1.0000x reference)
import numpy as np
import ml_dtypes
from contextlib import ExitStack

import concourse.bass as bass
import concourse.mybir as mybir
from concourse.bass_utils import run_bass_kernel_spmd

F32 = mybir.dt.float32
BF16 = mybir.dt.bfloat16
AF = mybir.ActivationFunctionType
ALU = mybir.AluOpType
AX = mybir.AxisListType

REAL_CFG = dict(cores=8, TC=2048, D=1024, E=12, HID=4096, NQ=4, HD=64, TGW=512)
ENGINES = ("tensor", "vector", "scalar", "gpsimd", "sync")


class Build:
    def __init__(self, nc):
        self.nc = nc
        self.P = {e: [] for e in ENGINES}
        self.S = {}
        self.cnt = {}

    def sem(self, name):
        self.S[name] = self.nc.alloc_semaphore(name)
        self.cnt[name] = 0

    def bump(self, name, n=1):
        self.cnt[name] += n
        return self.cnt[name]

    def wait(self, eng, sem, val):
        if not val or val <= 0:
            return
        h = self.S[sem]
        self.P[eng].append(lambda E, h=h, v=val: E.wait_ge(h, v))

    AUTO = {"vector": "ve", "scalar": "se"}

    def op(self, eng, meth, *a, inc=None, **kw):
        auto = self.AUTO.get(eng) if meth not in ("dma_start", "collective_compute") else None
        if auto is not None:
            prev = self.cnt[auto]
            if prev:
                self.wait(eng, auto, prev)
            if inc is None:
                inc = (auto, 1)
        v = self.bump(*inc) if inc else None
        h = self.S[inc[0]] if inc else None
        n = inc[1] if inc else None

        def f(E, meth=meth, a=a, kw=kw, h=h, n=n):
            r = getattr(E, meth)(*a, **kw)
            if h is not None:
                r.then_inc(h, n)
        self.P[eng].append(f)
        return v

    def mmgroup(self, mms, inc=None):
        v = self.bump(*inc) if inc else None
        h = self.S[inc[0]] if inc else None
        n = inc[1] if inc else None
        mms = [(("wait", self.S[m[1]], m[2]) if m[0] == "wait" else m) for m in mms]

        def f(T, mms=mms, h=h, n=n):
            ins = None
            for m in mms:
                if m[0] == "wait":
                    if m[2] > 0:
                        T.wait_ge(m[1], m[2])
                    continue
                (o, l, r, st, sp) = m
                ins = T.matmul(o, l, r, start=st, stop=sp)
            if h is not None:
                ins.then_inc(h, n)
        self.P["tensor"].append(f)
        return v


def build_nc(cfg=None, n_cores=None, stage=3):
    cfg = dict(cfg or REAL_CFG)
    cores = n_cores if n_cores is not None else cfg["cores"]
    TC, D, E, HID, NQ, HD, TGW = (cfg[k] for k in ("TC", "D", "E", "HID", "NQ", "HD", "TGW"))
    NT = TC // 128
    NTG = TC // TGW
    NSL = TGW // 128
    DCH = D // 128
    QH = HID // NQ
    QHC = QH // 128
    HC_ALL = HID // 128
    NNH = max(1, D // 512)
    NW = D // NNH

    nc = bass.Bass("TRN2", target_bir_lowering=False)

    x_h = nc.dram_tensor("x", [TC, D], F32, kind="ExternalInput")
    wq_h = nc.dram_tensor("wq", [D, HD], F32, kind="ExternalInput")
    wk_h = nc.dram_tensor("wk", [D, HD], F32, kind="ExternalInput")
    wv_h = nc.dram_tensor("wv", [D, HD], F32, kind="ExternalInput")
    wo_h = nc.dram_tensor("wo", [HD, D], F32, kind="ExternalInput")
    wg_h = nc.dram_tensor("wg", [D, E], F32, kind="ExternalInput")
    wup_h = nc.dram_tensor("wup", [E, D, HID], F32, kind="ExternalInput")
    bupT_h = nc.dram_tensor("bupT", [E, 128, HID // 128], F32, kind="ExternalInput")
    wdn_h = nc.dram_tensor("wdn", [E, HID, D], F32, kind="ExternalInput")
    bdn_h = nc.dram_tensor("bdn", [E, D], F32, kind="ExternalInput")
    rkm_h = nc.dram_tensor("rkm", [65, 8], F32, kind="ExternalInput")
    y_h = nc.dram_tensor("y", [TC, D], F32, kind="ExternalOutput")

    cc_in = nc.dram_tensor("cc_in", [65, HD], F32, kind="Internal")
    cc_out = nc.dram_tensor("cc_out", [65 * cores, HD], F32, kind="Internal")

    bfd = ml_dtypes.bfloat16
    ident_b = nc.inline_tensor(np.eye(128, dtype=bfd), name="ident_b")
    ident_f = nc.inline_tensor(np.eye(128, dtype=np.float32), name="ident_f")
    ut_f = nc.inline_tensor(np.triu(np.ones((128, 128), dtype=np.float32)), name="ut_f")
    ut_b = nc.inline_tensor(np.triu(np.ones((128, 128))).astype(bfd), name="ut_b")
    ones65 = nc.inline_tensor(np.ones((65, 128), dtype=bfd), name="ones65_b")
    onescol = nc.inline_tensor(np.ones((128, 1), dtype=bfd), name="onescol_b")
    sel_np = np.zeros((E, E * 128), dtype=bfd)
    for e in range(E):
        sel_np[e, e * 128:(e + 1) * 128] = 1.0
    sel_h = nc.inline_tensor(sel_np, name="sel_b")

    with ExitStack() as ctx:
        En = ctx.enter_context
        xres = En(nc.sbuf_tensor("xres", [128, NT * D], F32))
        xnt = En(nc.sbuf_tensor("xnt", [128, DCH * TC], BF16))
        xn = En(nc.sbuf_tensor("xn", [128, D], F32))
        wup_b = [En(nc.sbuf_tensor(f"wupb{i}", [128, DCH * QH], BF16)) for i in range(2)]
        wdn_b = [En(nc.sbuf_tensor(f"wdnb{i}", [128, QHC * D], BF16)) for i in range(2)]
        hT = En(nc.sbuf_tensor("hT", [128, QHC * TGW], BF16))
        cwB = En(nc.sbuf_tensor("cwB", [128, TGW], BF16))
        scrb = En(nc.sbuf_tensor("scrb", [128, 2 * TGW], BF16))
        cwT = En(nc.sbuf_tensor("cwT", [E, TC], BF16))
        cwt = En(nc.sbuf_tensor("cwt", [128, NT * E], F32))
        gwC = En(nc.sbuf_tensor("gwC", [128, NT * E], F32))
        gsc = En(nc.sbuf_tensor("gsc", [128, 64], F32))
        stw = En(nc.sbuf_tensor("stw", [128, 64], F32))
        scr = En(nc.sbuf_tensor("scr", [128, max(512, (cores + 1) * HD, 2 * NT * E)], F32))
        bupT_s = En(nc.sbuf_tensor("bupTs", [128, E * HC_ALL], F32))
        bdn_s = En(nc.sbuf_tensor("bdns", [E, D], BF16))
        sel_s = En(nc.sbuf_tensor("sels", [E, E * 128], BF16))
        wg_s = En(nc.sbuf_tensor("wgs", [128, DCH * E], BF16))
        wq_s = En(nc.sbuf_tensor("wqs", [128, DCH * HD], BF16))
        wk_s = En(nc.sbuf_tensor("wks", [128, DCH * HD], BF16))
        wv_s = En(nc.sbuf_tensor("wvs", [128, DCH * HD], BF16))
        wo_s = En(nc.sbuf_tensor("wos", [HD, D], BF16))
        qTc = En(nc.sbuf_tensor("qTc", [HD, 2 * 128], BF16))
        vTc = En(nc.sbuf_tensor("vTc", [HD, 2 * 128], BF16))
        kn = En(nc.sbuf_tensor("kn", [128, NT * HD], BF16))
        qn = En(nc.sbuf_tensor("qn", [128, 2 * HD], F32))
        vn = En(nc.sbuf_tensor("vn", [128, 2 * HD], BF16))
        Bm = En(nc.sbuf_tensor("Bm", [128, 2 * 128], BF16))
        osb = En(nc.sbuf_tensor("osb", [128, 2 * HD], BF16))
        oT = En(nc.sbuf_tensor("oT", [HD, 2 * 128], BF16))
        carr = En(nc.sbuf_tensor("carr", [65, (NT + 1) * HD], F32))
        carrb = En(nc.sbuf_tensor("carrb", [65, 2 * HD], BF16))
        pin = En(nc.sbuf_tensor("pin", [65, HD], F32))
        rkm_s = En(nc.sbuf_tensor("rkms", [65, 8], F32))
        ident_bs = En(nc.sbuf_tensor("identbs", [128, 128], BF16))
        ident_fs = En(nc.sbuf_tensor("identfs", [128, 128], F32))
        ut_fs = En(nc.sbuf_tensor("utfs", [128, 128], F32))
        ut_bs = En(nc.sbuf_tensor("utbs", [128, 128], BF16))
        ones65_s = En(nc.sbuf_tensor("ones65s", [65, 128], BF16))
        onescol_s = En(nc.sbuf_tensor("onescols", [128, 1], BF16))
        tqz = En(nc.sbuf_tensor("tqz", [128, HD], F32))

        pU = [En(nc.psum_tensor(f"pU{i}", [128, 512], F32)) for i in range(2)]
        pDA = En(nc.psum_tensor("pDA", [128, 1024], F32))
        pDB = En(nc.psum_tensor("pDB", [128, 1024], F32))
        pW = En(nc.psum_tensor("pW", [128, 512], F32))
        pT = En(nc.psum_tensor("pT", [128, 128], BF16))
        pD = [pDA, pDB]

        b = Build(nc)
        for s in ("xd", "ld", "wd0", "wd1", "stg", "st", "cc", "ccd", "pe", "ve", "se"):
            b.sem(s)

        xr3 = x_h.ap().rearrange("(i p) d -> p i d", p=128)
        step = max(1, NT // 4)
        for j in range(0, NT, step):
            hi = min(NT, j + step)
            b.op("sync", "dma_start",
                 out=xres[:, :].rearrange("p (i d) -> p i d", d=D)[:, j:hi, :],
                 in_=xr3[:, j:hi, :], inc=("xd", 16))
        for dst, src in ((ident_bs, ident_b), (ident_fs, ident_f), (ut_fs, ut_f),
                         (ut_bs, ut_b), (ones65_s, ones65), (onescol_s, onescol),
                         (sel_s, sel_h), (rkm_s, rkm_h)):
            b.op("sync", "dma_start", out=dst[:, :], in_=src.ap(), inc=("xd", 16))
        b.op("sync", "dma_start",
             out=bupT_s[:, :].rearrange("p (e c) -> p e c", e=E),
             in_=bupT_h.ap().rearrange("e p c -> p e c"), inc=("xd", 16))
        XD_ALL = b.cnt["xd"]

        b.op("gpsimd", "dma_start",
             out=wg_s[:, :].rearrange("p (k g) -> p k g", g=E),
             in_=wg_h.ap().rearrange("(k p) g -> p k g", p=128), inc=("ld", 16))
        for (dst, src) in ((wq_s, wq_h), (wk_s, wk_h), (wv_s, wv_h)):
            b.op("gpsimd", "dma_start",
                 out=dst[:, :].rearrange("p (k h) -> p k h", h=HD),
                 in_=src.ap().rearrange("(k p) h -> p k h", p=128), inc=("ld", 16))
        b.op("gpsimd", "dma_start", out=wo_s[:, :], in_=wo_h.ap(), inc=("ld", 16))
        b.op("gpsimd", "dma_start", out=bdn_s[:, :], in_=bdn_h.ap(), inc=("ld", 16))
        LD_ALL = b.cnt["ld"]

        TR = {"evac": 0, "xn_free": {0: 0, 1: 0}, "ln_ev": {0: 0, 1: 0}}
        LNST = {"std": {0: 0, 1: 0}}
        GST = {"exp": {0: 0}, "tr": {0: 0}}
        GL = {"ev": 0}

        def gate_logits_cb(i, tile_ve):
            gwL = scr[:, 0:NT * E]
            if i == 0:
                b.wait("tensor", "ld", LD_ALL)
                b.wait("tensor", "ve", CWB["evac"])
            b.wait("tensor", "ve", GL["ev"])
            b.wait("tensor", "ve", tile_ve)
            mms = [(pW[:, 0:E], xnt[:, k * TC + i * 128: k * TC + (i + 1) * 128],
                    wg_s[:, k * E:(k + 1) * E], k == 0, k == DCH - 1) for k in range(DCH)]
            vmm = b.mmgroup(mms, inc=("pe", 1))
            b.wait("vector", "pe", vmm)
            if i == 0:
                b.wait("vector", "se", GST["exp"][0])
            GL["ev"] = b.op("vector", "tensor_copy", gwL[:, i * E:(i + 1) * E],
                            pW[0:128, 0:E], inc=("ve", 1))
        CWB = {"evac": 0}
        W = {"gq": 0, "qres": []}

        def gen_ln(sq_wait, xnt_busy_pe, scr_free_ve=0, pu_free_se=0, pu_free_ve=0, tile_cb=None):
            ve_done = 0
            NBN = (D + 511) // 512
            for i in range(NT):
                par = i % 2
                sb = par * 32
                xt = xres[:, i * D:(i + 1) * D]
                if i == 0:
                    b.wait("vector", "xd", XD_ALL)
                b.wait("vector", "se", LNST["std"][par])
                for j in range(NBN):
                    w0 = j * 512
                    w1 = min(D, w0 + 512)
                    b.op("vector", "bn_stats", stw[:, sb + 16 + 6 * j: sb + 22 + 6 * j],
                         xt[:, w0:w1])
                b.op("vector", "bn_aggr", stw[:, sb:sb + 2], stw[:, sb + 16: sb + 16 + 6 * NBN])
                vvar = b.op("vector", "tensor_scalar_add", stw[:, sb + 1:sb + 2],
                            stw[:, sb + 1:sb + 2], 1e-5, inc=("ve", 1))
                b.wait("scalar", "ve", vvar)
                vstd = b.op("scalar", "activation", stw[:, sb + 4:sb + 5], stw[:, sb + 1:sb + 2], AF.Sqrt, inc=("se", 1))
                LNST["std"][par] = vstd
                b.wait("vector", "se", vstd)
                b.op("vector", "reciprocal", stw[:, sb + 5:sb + 6], stw[:, sb + 4:sb + 5])
                b.wait("vector", "pe", TR["xn_free"][0])
                vno = b.op("vector", "tensor_scalar", xn[:, 0:D], xt,
                           stw[:, sb:sb + 1], stw[:, sb + 5:sb + 6],
                           ALU.subtract, ALU.mult, inc=("ve", 1))
                b.wait("tensor", "ve", vno)
                if i == 0:
                    b.wait("tensor", "xd", XD_ALL)
                    b.wait("tensor", "se", pu_free_se)
                    b.wait("tensor", "ve", pu_free_ve)
                for k in range(DCH):
                    bk2 = k % 2
                    b.wait("tensor", "ve", TR["ln_ev"][bk2])
                    vtr = b.op("tensor", "transpose", pU[bk2][:, 0:128],
                               xn[:, k * 128:(k + 1) * 128],
                               ident_fs[:, :], inc=("pe", 1))
                    if k == DCH - 1:
                        TR["xn_free"][0] = vtr
                    b.wait("vector", "pe", vtr)
                    if i == 0 and k == 0:
                        b.wait("vector", "pe", xnt_busy_pe)
                    TR["ln_ev"][bk2] = b.op("vector", "tensor_copy",
                                            xnt[:, k * TC + i * 128: k * TC + (i + 1) * 128],
                                            pU[bk2][0:128, 0:128], inc=("ve", 1))
                ve_done = TR["ln_ev"][(DCH - 1) % 2]
                TR["evac"] = max(TR["ln_ev"][0], TR["ln_ev"][1])
                if tile_cb is not None:
                    tile_cb(i, max(TR["ln_ev"][0], TR["ln_ev"][1]))
            return max(TR["ln_ev"][0], TR["ln_ev"][1])

        def gen_gate(xnt_ve):
            GE = NT * E
            gwL = scr[:, 0:GE]
            gwB = scr[:, GE:2 * GE]
            m1 = gsc[:, 0:NT]
            m2r = gsc[:, 16:16 + NT]
            m3r = gsc[:, 32:32 + NT]
            rsg = gsc[:, 48:48 + NT]

            def seg(ap):
                return ap.rearrange("p (g e) -> p g e", e=E)

            def bc(ap_col):
                base = ap_col
                return bass.AP(base.tensor, base.offset,
                               [[base.ap[0][0], 128], [1, NT], [0, E]])

            b.op("vector", "tensor_reduce", m1[:, :].rearrange("p (g o) -> p g o", o=1),
                 seg(gwL), axis=AX.X, op=ALU.max)
            b.op("vector", "tensor_tensor", seg(gwC[:, :]), seg(gwL), bc(m1), ALU.is_ge)
            b.op("vector", "scalar_tensor_tensor", gwB, gwC[:, :], -1e30, gwL[:, 0:GE],
                 ALU.mult, ALU.add)
            b.op("vector", "tensor_reduce", m2r[:, :].rearrange("p (g o) -> p g o", o=1),
                 seg(gwB), axis=AX.X, op=ALU.max)
            b.op("vector", "tensor_tensor", seg(gwC[:, :]), seg(gwB), bc(m2r), ALU.is_ge)
            b.op("vector", "scalar_tensor_tensor", gwB, gwC[:, :], -1e30, gwB,
                 ALU.mult, ALU.add)
            b.op("vector", "tensor_reduce", m3r[:, :].rearrange("p (g o) -> p g o", o=1),
                 seg(gwB), axis=AX.X, op=ALU.max)
            vmask = b.op("vector", "tensor_tensor", seg(gwC[:, :]), seg(gwL), bc(m3r),
                         ALU.is_ge, inc=("ve", 1))
            vlc = b.op("vector", "tensor_tensor", seg(gwL), seg(gwL), bc(m1), ALU.subtract)
            b.wait("scalar", "ve", vlc)
            vexp = b.op("scalar", "activation", gwB, gwL[:, 0:GE], AF.Exp, inc=("se", 1))
            GST["exp"][0] = vexp
            b.wait("vector", "se", vexp)
            b.op("vector", "tensor_mul", gwB, gwB, gwC[:, :])
            b.op("vector", "tensor_reduce", rsg[:, :].rearrange("p (g o) -> p g o", o=1),
                 seg(gwB), axis=AX.X, op=ALU.add)
            b.op("vector", "reciprocal", rsg[:, :], rsg[:, :])
            b.op("vector", "tensor_tensor", seg(gwB), seg(gwB), bc(rsg), ALU.mult)
            b.wait("vector", "pe", GST["tr"][0])
            vcw = b.op("vector", "tensor_scalar_mul", cwt[:, 0:GE], gwB, 0.5, inc=("ve", 1))
            prev_evac = 0
            ev = [0, 0]
            for i in range(NT):
                bk = i % 2
                b.wait("tensor", "ve", vcw if i == 0 else 0)
                if ev[bk]:
                    b.wait("tensor", "ve", ev[bk])
                vtr = b.op("tensor", "transpose", pU[bk][0:E, 0:128],
                           cwt[:, i * E:(i + 1) * E], ident_fs[:, :], inc=("pe", 1))
                GST["tr"][0] = vtr
                b.wait("vector", "pe", vtr)
                ev[bk] = b.op("vector", "tensor_copy", cwT[0:E, i * 128:(i + 1) * 128],
                              pU[bk][0:E, 0:128], inc=("ve", 1))
                prev_evac = ev[bk]
            return max(ev[0], ev[1])

        def emit_wdma(e, q):
            gq = W["gq"]
            par = gq % 2
            if gq >= 2:
                b.wait("gpsimd", "ve", W["qres"][gq - 2])
            wdsem = f"wd{par}"
            b.op("gpsimd", "dma_start",
                 out=wup_b[par][:, :].rearrange("p (k h) -> p k h", h=QH),
                 in_=wup_h.ap()[e, :, q * QH:(q + 1) * QH].rearrange("(k p) h -> p k h", p=128),
                 inc=(wdsem, 16))
            b.op("gpsimd", "dma_start",
                 out=wdn_b[par][:, :].rearrange("p (c d) -> p c d", d=D),
                 in_=wdn_h.ap()[e, q * QH:(q + 1) * QH, :].rearrange("(c p) d -> p c d", p=128),
                 inc=(wdsem, 16))
            W["gq"] += 1
            return gq

        def gen_ffn(xnt_ve, cw_ve, scr_free_ve, pre_waits):
            gelu_hist, cwm_hist, res_hist = [], [], []
            slice_res = [0] * NT
            dn_cnt = 0
            first = True
            dnlast = 0
            for e in range(E):
                for q in range(NQ):
                    gq = emit_wdma(e, q)
                    par = gq % 2
                    b.wait("tensor", f"wd{gq % 2}", 32 * (gq // 2 + 1))
                    if first:
                        b.wait("tensor", "ve", cw_ve)
                        b.wait("tensor", "ve", xnt_ve)
                        for (s_, v_) in pre_waits:
                            b.wait("tensor", s_, v_)
                        first = False
                    for tg in range(NTG):
                        b.wait("tensor", "ve", CWB["evac"])
                        vcwb = b.op("tensor", "matmul", pW[:, 0:TGW],
                                    sel_s[:, e * 128:(e + 1) * 128],
                                    cwT[0:E, tg * TGW:(tg + 1) * TGW],
                                    start=True, stop=True, inc=("pe", 1))
                        b.wait("vector", "pe", vcwb)
                        CWB["evac"] = b.op("vector", "tensor_copy", cwB[:, 0:TGW], pW[:, 0:TGW],
                                           inc=("ve", 1))
                        for hc in range(QHC):
                            g = len(gelu_hist)
                            bk = g % 2
                            if g >= 2:
                                b.wait("tensor", "se", gelu_hist[-2])
                            mms = [(pU[bk][:, 0:TGW],
                                    wup_b[par][:, k * QH + hc * 128: k * QH + (hc + 1) * 128],
                                    xnt[:, k * TC + tg * TGW: k * TC + (tg + 1) * TGW],
                                    k == 0, k == DCH - 1) for k in range(DCH)]
                            vup = b.mmgroup(mms, inc=("pe", 1))
                            b.wait("scalar", "pe", vup)
                            b.wait("scalar", "ve",
                                   cwm_hist[-2] if len(cwm_hist) >= 2 else scr_free_ve)
                            bias = bupT_s[:, e * HC_ALL + q * QHC + hc: e * HC_ALL + q * QHC + hc + 1]
                            vg = b.op("scalar", "activation", scrb[:, bk * TGW:(bk + 1) * TGW],
                                      pU[bk][:, 0:TGW], AF.Gelu, bias=bias, inc=("se", 1))
                            gelu_hist.append(vg)
                            b.wait("vector", "se", vg)
                            b.wait("vector", "pe", dnlast)
                            vcm = b.op("vector", "tensor_mul", hT[:, hc * TGW:(hc + 1) * TGW],
                                       scrb[:, bk * TGW:(bk + 1) * TGW], cwB[:, 0:TGW],
                                       inc=("ve", 1))
                            cwm_hist.append(vcm)
                        tg_cwm = cwm_hist[-QHC:]
                        for sl in range(NSL):
                            tile = tg * NSL + sl
                            bk = dn_cnt % 2
                            mms = []
                            if e == 0 and q == 0:
                                for nh in range(NNH):
                                    mms.append((pD[bk][:, nh * NW:(nh + 1) * NW],
                                                cwT[0:E, tile * 128:(tile + 1) * 128],
                                                bdn_s[0:E, nh * NW:(nh + 1) * NW], True, False))
                            for c in range(QHC):
                                if sl == 0:
                                    mms.append(("wait", "ve", tg_cwm[c]))
                                st_ = (c == 0) and not (e == 0 and q == 0)
                                for nh in range(NNH):
                                    mms.append((pD[bk][:, nh * NW:(nh + 1) * NW],
                                                hT[:, c * TGW + sl * 128: c * TGW + (sl + 1) * 128],
                                                wdn_b[par][:, c * D + nh * NW: c * D + (nh + 1) * NW],
                                                st_, c == QHC - 1))
                            if sl > 0:
                                b.wait("tensor", "ve", tg_cwm[-1])
                            if len(res_hist) >= 2:
                                b.wait("tensor", "ve", res_hist[-2])
                            vdn = b.mmgroup(mms, inc=("pe", 1))
                            dnlast = vdn
                            dn_cnt += 1
                            b.wait("vector", "pe", vdn)
                            xt = xres[:, tile * D:(tile + 1) * D]
                            vres = b.op("vector", "tensor_add", xt, pD[bk][:, 0:D], xt, inc=("ve", 1))
                            res_hist.append(vres)
                            slice_res[tile] = vres
                    W["qres"].append(res_hist[-1])
            return slice_res, gelu_hist[-1], cwm_hist[-1]

        ln1_ve = gen_ln([0] * NT, 0, tile_cb=gate_logits_cb)
        cw1_ve = gen_gate(ln1_ve)
        CWB["evac"] = max(CWB["evac"], GL["ev"])
        res1, gelu1_se, cwm1_ve = gen_ffn(ln1_ve, cw1_ve, 0, [])
        final_res = res1

        if stage >= 2:
            pe_f1 = b.cnt["pe"]
            maxres1 = max(res1)
            ln2_ve = gen_ln(res1, pe_f1, cwm1_ve, pu_free_se=gelu1_se)

            b.op("vector", "memset", carr[:, 0:HD], 0.0)
            pack_ve = 0
            vn_ev = 0
            kn_ve = [0] * NT
            knf = {0: 0, 1: 0}
            vnf = {0: maxres1, 1: maxres1}
            vn_evs = [0] * NT
            for c in range(NT):
                bk = c % 2
                if c == 0:
                    b.wait("tensor", "ve", ln2_ve)
                    b.wait("tensor", "se", gelu1_se)
                b.wait("tensor", "ve", knf[bk])
                mms = [(pU[bk][:, 0:HD], xnt[:, k * TC + c * 128: k * TC + (c + 1) * 128],
                        wk_s[:, k * HD:(k + 1) * HD], k == 0, k == DCH - 1) for k in range(DCH)]
                vk = b.mmgroup(mms, inc=("pe", 1))
                b.wait("tensor", "ve", vnf[bk])
                mms = [(pD[bk][:, 0:HD], xnt[:, k * TC + c * 128: k * TC + (c + 1) * 128],
                        wv_s[:, k * HD:(k + 1) * HD], k == 0, k == DCH - 1) for k in range(DCH)]
                vv = b.mmgroup(mms, inc=("pe", 1))
                b.wait("vector", "pe", vk)
                b.op("vector", "tensor_scalar_min", scr[:, 0:HD], pU[bk][:, 0:HD], 0.0)
                vmx = b.op("vector", "tensor_scalar_max", scr[:, HD:2 * HD], pU[bk][:, 0:HD], 0.0,
                           inc=("ve", 1))
                knf[bk] = vmx
                b.wait("scalar", "ve", vmx)
                vex = b.op("scalar", "activation", scr[:, 0:HD], scr[:, 0:HD], AF.Exp, inc=("se", 1))
                b.wait("vector", "se", vex)
                kn_ve[c] = b.op("vector", "tensor_add", kn[:, c * HD:(c + 1) * HD],
                                scr[:, 0:HD], scr[:, HD:2 * HD], inc=("ve", 1))
                b.wait("vector", "pe", vv)
                vn_ev = b.op("vector", "tensor_copy", vn[:, bk * HD:(bk + 1) * HD],
                             pD[bk][:, 0:HD], inc=("ve", 1))
                vnf[bk] = vn_ev
                vn_evs[c] = vn_ev
                b.wait("tensor", "ve", vn_ev)
                b.wait("tensor", "ve", pack_ve if pack_ve else CWB["evac"])
                mms = [(pW[0:HD, 0:HD], vn[:, bk * HD:(bk + 1) * HD],
                        kn[:, c * HD:(c + 1) * HD], True, True),
                       (pW[64:65, 0:HD], onescol_s[:, :], kn[:, c * HD:(c + 1) * HD], True, True)]
                vpk = b.mmgroup(mms, inc=("pe", 1))
                b.wait("vector", "pe", vpk)
                pack_ve = b.op("vector", "tensor_add", carr[:, (c + 1) * HD:(c + 2) * HD],
                               pW[0:65, 0:HD], carr[:, c * HD:(c + 1) * HD], inc=("ve", 1))
            CWB["evac"] = pack_ve

            b.wait("scalar", "ve", pack_ve)
            b.op("scalar", "dma_start", out=cc_in.ap(), in_=carr[:, NT * HD:(NT + 1) * HD],
                 inc=("stg", 16))
            b.wait("gpsimd", "stg", b.cnt["stg"])
            vcc = b.op("gpsimd", "collective_compute", "AllGather", ALU.bypass,
                       replica_groups=[list(range(cores))],
                       ins=[cc_in.ap().opt()], outs=[cc_out.ap().opt()], inc=("cc", 1))
            b.wait("gpsimd", "cc", vcc)
            b.wait("gpsimd", "ve", pack_ve)
            b.op("gpsimd", "dma_start",
                 out=scr[0:65, 0:cores * HD].rearrange("p (r h) -> p r h", h=HD),
                 in_=cc_out.ap().rearrange("(r p) h -> p r h", p=65), inc=("ccd", 16))
            b.wait("vector", "ccd", b.cnt["ccd"])
            b.op("vector", "memset", pin[:, :], 0.0)
            for r in range(cores):
                b.op("vector", "scalar_tensor_tensor", pin[0:65, :],
                     scr[0:65, r * HD:(r + 1) * HD], rkm_s[:, r:r + 1], pin[0:65, :],
                     ALU.mult, ALU.add)

            o_se = 0
            den_ve = 0
            ao_res = 0
            res2 = [0] * NT
            vmsk_arr = [0] * NT
            prev_vtc = 0
            prev_vmsk = 0
            otr = {0: 0, 1: 0}
            attn_scr_last = 0

            def stageA(c):
                nonlocal prev_vtc, prev_vmsk, attn_scr_last
                p2 = c % 2
                b.wait("tensor", "ve", prev_vtc if prev_vtc else vn_ev)
                mms = [(pU[0][0:HD, 0:128], wq_s[:, k * HD:(k + 1) * HD],
                        xnt[:, k * TC + c * 128: k * TC + (c + 1) * 128],
                        k == 0, k == DCH - 1) for k in range(DCH)]
                vqt = b.mmgroup(mms, inc=("pe", 1))
                b.wait("vector", "pe", vqt)
                b.op("vector", "tensor_scalar_min", scr[0:HD, 0:128], pU[0][0:HD, 0:128], 0.0)
                vmx = b.op("vector", "tensor_scalar_max", scr[0:HD, 128:256],
                           pU[0][0:HD, 0:128], 0.0, inc=("ve", 1))
                b.wait("scalar", "ve", vmx)
                vex = b.op("scalar", "activation", scr[0:HD, 0:128], scr[0:HD, 0:128],
                           AF.Exp, inc=("se", 1))
                b.wait("vector", "se", vex)
                vq_add = b.op("vector", "tensor_add", qTc[0:HD, p2 * 128:(p2 + 1) * 128],
                              scr[0:HD, 0:128], scr[0:HD, 128:256], inc=("ve", 1))
                b.wait("tensor", "ve", prev_vmsk if prev_vmsk else kn_ve[NT - 1])
                mms = [(pU[1][:, 0:HD], xnt[:, k * TC + c * 128: k * TC + (c + 1) * 128],
                        wq_s[:, k * HD:(k + 1) * HD], k == 0, k == DCH - 1) for k in range(DCH)]
                vqn = b.mmgroup(mms, inc=("pe", 1))
                b.wait("vector", "pe", vqn)
                b.op("vector", "tensor_scalar_min", scr[:, 0:HD], pU[1][:, 0:HD], 0.0)
                vmx = b.op("vector", "tensor_scalar_max", scr[:, HD:2 * HD], pU[1][:, 0:HD], 0.0,
                           inc=("ve", 1))
                b.wait("scalar", "ve", vmx)
                vex = b.op("scalar", "activation", scr[:, 0:HD], scr[:, 0:HD], AF.Exp, inc=("se", 1))
                b.wait("vector", "se", vex)
                vqn_add = b.op("vector", "tensor_add", qn[:, p2 * HD:(p2 + 1) * HD],
                               scr[:, 0:HD], scr[:, HD:2 * HD], inc=("ve", 1))
                attn_scr_last = vqn_add
                b.wait("tensor", "ve", vq_add)
                mms = [(pU[0][0:HD, 0:128], wv_s[:, k * HD:(k + 1) * HD],
                        xnt[:, k * TC + c * 128: k * TC + (c + 1) * 128],
                        k == 0, k == DCH - 1) for k in range(DCH)]
                vvt = b.mmgroup(mms, inc=("pe", 1))
                b.wait("vector", "pe", vvt)
                prev_vtc = b.op("vector", "tensor_copy", vTc[0:HD, p2 * 128:(p2 + 1) * 128],
                                pU[0][0:HD, 0:128], inc=("ve", 1))
                b.wait("tensor", "ve", prev_vtc)
                vB = b.op("tensor", "matmul", pU[1][:, 0:128],
                          vTc[0:HD, p2 * 128:(p2 + 1) * 128],
                          qTc[0:HD, p2 * 128:(p2 + 1) * 128],
                          start=True, stop=True, inc=("pe", 1))
                b.wait("vector", "pe", vB)
                prev_vmsk = b.op("vector", "tensor_mul", Bm[:, p2 * 128:(p2 + 1) * 128],
                                 pU[1][:, 0:128], ut_fs[:, :], inc=("ve", 1))
                vmsk_arr[c] = prev_vmsk

            def stageB(c):
                nonlocal o_se, den_ve, ao_res
                p2 = c % 2
                vcb = b.op("vector", "tensor_add", carrb[:, p2 * HD:(p2 + 1) * HD],
                           carr[:, c * HD:(c + 1) * HD], pin[0:65, :], inc=("ve", 1))
                b.wait("tensor", "ve", vcb)
                b.wait("tensor", "se", o_se)
                if c < 2:
                    b.wait("tensor", "ve", max(maxres1, vn_evs[NT - 2 + c]))
                mms = [(pDA[:, 0:HD], Bm[:, p2 * 128:(p2 + 1) * 128],
                        kn[:, c * HD:(c + 1) * HD], True, False),
                       (pDA[:, 0:HD], qTc[0:HD, p2 * 128:(p2 + 1) * 128],
                        carrb[0:HD, p2 * HD:(p2 + 1) * HD], False, True)]
                vO = b.mmgroup(mms, inc=("pe", 1))
                b.wait("tensor", "ve", den_ve)
                mms = [(pDA[:, 512:512 + HD], ut_bs[:, :], kn[:, c * HD:(c + 1) * HD], True, False),
                       (pDA[:, 512:512 + HD], ones65_s[64:65, 0:128],
                        carrb[64:65, p2 * HD:(p2 + 1) * HD], False, True)]
                vZ = b.mmgroup(mms, inc=("pe", 1))
                b.wait("vector", "pe", vZ)
                b.wait("vector", "se", o_se)
                b.op("vector", "tensor_mul", tqz[:, 0:HD], pDA[:, 512:512 + HD],
                     qn[:, p2 * HD:(p2 + 1) * HD])
                b.op("vector", "reduce_sum", stw[:, 14:15], tqz[:, 0:HD], axis=AX.X)
                b.op("vector", "tensor_scalar_max", stw[:, 14:15], stw[:, 14:15], 1e-6)
                den_ve = b.op("vector", "reciprocal", stw[:, 15:16], stw[:, 14:15], inc=("ve", 1))
                b.wait("scalar", "ve", den_ve)
                b.wait("scalar", "pe", vO)
                b.wait("scalar", "pe", otr[p2])
                o_se = b.op("scalar", "activation", osb[:, p2 * HD:(p2 + 1) * HD], pDA[:, 0:HD],
                            AF.Copy, scale=stw[:, 15:16], inc=("se", 1))
                b.wait("tensor", "se", o_se)
                b.wait("tensor", "ve", TR["evac"])
                vtr = b.op("tensor", "transpose", pT[0:HD, 0:128], osb[:, p2 * HD:(p2 + 1) * HD],
                           ident_bs[:, :], inc=("pe", 1))
                otr[p2] = vtr
                b.wait("vector", "pe", vtr)
                TR["evac"] = b.op("vector", "tensor_copy", oT[0:HD, p2 * 128:(p2 + 1) * 128],
                                  pT[0:HD, 0:128], inc=("ve", 1))
                b.wait("tensor", "ve", TR["evac"])
                b.wait("tensor", "ve", ao_res if ao_res else max(maxres1, vn_evs[NT - 1]))
                mms = [(pDB[:, nh * NW:(nh + 1) * NW], oT[0:HD, p2 * 128:(p2 + 1) * 128],
                        wo_s[0:HD, nh * NW:(nh + 1) * NW], True, True) for nh in range(NNH)]
                vAO = b.mmgroup(mms, inc=("pe", 1))
                b.wait("vector", "pe", vAO)
                xt = xres[:, c * D:(c + 1) * D]
                ao_res = b.op("vector", "tensor_add", xt, pDB[:, 0:D], xt, inc=("ve", 1))
                res2[c] = ao_res

            stageA(0)
            for c in range(NT):
                if c + 1 < NT:
                    stageA(c + 1)
                stageB(c)
            final_res = res2


        if stage >= 3:
            CWB["evac"] = max(CWB["evac"], pack_ve)
            ln3_ve = gen_ln(res2, b.cnt["pe"], attn_scr_last, pu_free_ve=max(res2),
                            tile_cb=gate_logits_cb)
            cw3_ve = gen_gate(ln3_ve)
            pre = [("se", o_se), ("ve", ao_res), ("ve", den_ve)]
            CWB["evac"] = max(CWB["evac"], GL["ev"])
            res3, _, _ = gen_ffn(ln3_ve, cw3_ve, attn_scr_last, pre)
            final_res = res3

        b.wait("sync", "ve", max(final_res))
        yr3 = y_h.ap().rearrange("(i p) d -> p i d", p=128)
        for j in range(0, NT, step):
            hi = min(NT, j + step)
            b.op("sync", "dma_start",
                 out=yr3[:, j:hi, :],
                 in_=xres[:, :].rearrange("p (i d) -> p i d", d=D)[:, j:hi, :], inc=("st", 16))
        b.wait("sync", "st", b.cnt["st"])

        import os as _os
        if _os.environ.get("KBUILD_DEBUG"):
            print("SBUF remaining:", nc.sbuf_bytes_remaining, "PSUM banks left:", nc.psum_banks_remaining)
        with nc.Block() as block:
            @block.tensor
            def _(t):
                for f in b.P["tensor"]:
                    f(t)

            @block.vector
            def _(v):
                for f in b.P["vector"]:
                    f(v)

            @block.scalar
            def _(s):
                for f in b.P["scalar"]:
                    f(s)

            @block.gpsimd
            def _(g):
                for f in b.P["gpsimd"]:
                    f(g)

            @block.sync
            def _(sy):
                for f in b.P["sync"]:
                    f(sy)

    return nc


_build_cache = {}


def get_nc(cfg=None, n_cores=None, stage=3):
    key = (tuple(sorted((cfg or REAL_CFG).items())), n_cores, stage)
    if key not in _build_cache:
        _build_cache[key] = build_nc(dict(cfg or REAL_CFG), n_cores, stage)
    return _build_cache[key]


def make_in_maps(inputs, cfg=None, n_cores=8):
    cfg = dict(cfg or REAL_CFG)
    TC, D, E, HID = cfg["TC"], cfg["D"], cfg["E"], cfg["HID"]
    x = np.ascontiguousarray(np.asarray(inputs["L0_attn_x"], dtype=np.float32).reshape(-1, D))
    bup = np.asarray(inputs["L0_ffn_bup"], dtype=np.float32)
    bupT = np.ascontiguousarray(bup.reshape(E, HID // 128, 128).transpose(0, 2, 1))
    common = dict(
        wq=np.ascontiguousarray(np.asarray(inputs["L0_attn_W_q"], dtype=np.float32)),
        wk=np.ascontiguousarray(np.asarray(inputs["L0_attn_W_k"], dtype=np.float32)),
        wv=np.ascontiguousarray(np.asarray(inputs["L0_attn_W_v"], dtype=np.float32)),
        wo=np.ascontiguousarray(np.asarray(inputs["L0_attn_W_o"], dtype=np.float32)),
        wg=np.ascontiguousarray(np.asarray(inputs["L0_ffn_Wg"], dtype=np.float32)),
        wup=np.ascontiguousarray(np.asarray(inputs["L0_ffn_Wup"], dtype=np.float32)),
        bupT=bupT,
        wdn=np.ascontiguousarray(np.asarray(inputs["L0_ffn_Wdown"], dtype=np.float32)),
        bdn=np.ascontiguousarray(np.asarray(inputs["L0_ffn_bdown"], dtype=np.float32)),
    )
    in_maps = []
    for c in range(n_cores):
        rkm = np.zeros((65, 8), dtype=np.float32)
        rkm[:, :c] = 1.0
        m = dict(common)
        m["x"] = np.ascontiguousarray(x[c * TC:(c + 1) * TC])
        m["rkm"] = rkm
        in_maps.append(m)
    return in_maps


def kernel(**inputs):
    nc = get_nc()
    in_maps = make_in_maps(inputs)
    res = run_bass_kernel_spmd(nc, in_maps, core_ids=list(range(8)))
    out = np.concatenate([r["y"] for r in res.results], axis=0)
    return out.reshape(1, 16384, 1024)


# version 07
# speedup vs baseline: 1.0032x; 1.0032x over previous
import numpy as np
import ml_dtypes
from contextlib import ExitStack

import concourse.bass as bass
import concourse.mybir as mybir
from concourse.bass_utils import run_bass_kernel_spmd

F32 = mybir.dt.float32
BF16 = mybir.dt.bfloat16
AF = mybir.ActivationFunctionType
ALU = mybir.AluOpType
AX = mybir.AxisListType

REAL_CFG = dict(cores=8, TC=2048, D=1024, E=12, HID=4096, NQ=4, HD=64, TGW=512)
ENGINES = ("tensor", "vector", "scalar", "gpsimd", "sync")


class Build:
    def __init__(self, nc):
        self.nc = nc
        self.P = {e: [] for e in ENGINES}
        self.S = {}
        self.cnt = {}

    def sem(self, name):
        self.S[name] = self.nc.alloc_semaphore(name)
        self.cnt[name] = 0

    def bump(self, name, n=1):
        self.cnt[name] += n
        return self.cnt[name]

    def wait(self, eng, sem, val):
        if not val or val <= 0:
            return
        h = self.S[sem]
        self.P[eng].append(lambda E, h=h, v=val: E.wait_ge(h, v))

    AUTO = {"vector": "ve", "scalar": "se"}

    def op(self, eng, meth, *a, inc=None, **kw):
        auto = self.AUTO.get(eng) if meth not in ("dma_start", "collective_compute") else None
        if auto is not None:
            prev = self.cnt[auto]
            if prev:
                self.wait(eng, auto, prev)
            if inc is None:
                inc = (auto, 1)
        v = self.bump(*inc) if inc else None
        h = self.S[inc[0]] if inc else None
        n = inc[1] if inc else None

        def f(E, meth=meth, a=a, kw=kw, h=h, n=n):
            r = getattr(E, meth)(*a, **kw)
            if h is not None:
                r.then_inc(h, n)
        self.P[eng].append(f)
        return v

    def mmgroup(self, mms, inc=None):
        v = self.bump(*inc) if inc else None
        h = self.S[inc[0]] if inc else None
        n = inc[1] if inc else None
        mms = [(("wait", self.S[m[1]], m[2]) if m[0] == "wait" else m) for m in mms]

        def f(T, mms=mms, h=h, n=n):
            ins = None
            for m in mms:
                if m[0] == "wait":
                    if m[2] > 0:
                        T.wait_ge(m[1], m[2])
                    continue
                (o, l, r, st, sp) = m
                ins = T.matmul(o, l, r, start=st, stop=sp)
            if h is not None:
                ins.then_inc(h, n)
        self.P["tensor"].append(f)
        return v


def build_nc(cfg=None, n_cores=None, stage=3):
    cfg = dict(cfg or REAL_CFG)
    cores = n_cores if n_cores is not None else cfg["cores"]
    TC, D, E, HID, NQ, HD, TGW = (cfg[k] for k in ("TC", "D", "E", "HID", "NQ", "HD", "TGW"))
    NT = TC // 128
    NTG = TC // TGW
    NSL = TGW // 128
    DCH = D // 128
    QH = HID // NQ
    QHC = QH // 128
    HC_ALL = HID // 128
    NNH = max(1, D // 512)
    NW = D // NNH

    nc = bass.Bass("TRN2", target_bir_lowering=False)

    x_h = nc.dram_tensor("x", [TC, D], F32, kind="ExternalInput")
    wq_h = nc.dram_tensor("wq", [D, HD], F32, kind="ExternalInput")
    wk_h = nc.dram_tensor("wk", [D, HD], F32, kind="ExternalInput")
    wv_h = nc.dram_tensor("wv", [D, HD], F32, kind="ExternalInput")
    wo_h = nc.dram_tensor("wo", [HD, D], F32, kind="ExternalInput")
    wg_h = nc.dram_tensor("wg", [D, E], F32, kind="ExternalInput")
    wup_h = nc.dram_tensor("wup", [E, D, HID], F32, kind="ExternalInput")
    bupT_h = nc.dram_tensor("bupT", [E, 128, HID // 128], F32, kind="ExternalInput")
    wdn_h = nc.dram_tensor("wdn", [E, HID, D], F32, kind="ExternalInput")
    bdn_h = nc.dram_tensor("bdn", [E, D], F32, kind="ExternalInput")
    rkm_h = nc.dram_tensor("rkm", [65, 8], F32, kind="ExternalInput")
    y_h = nc.dram_tensor("y", [TC, D], F32, kind="ExternalOutput")

    cc_in = nc.dram_tensor("cc_in", [65, HD], F32, kind="Internal")
    cc_out = nc.dram_tensor("cc_out", [65 * cores, HD], F32, kind="Internal")

    bfd = ml_dtypes.bfloat16
    ident_b = nc.inline_tensor(np.eye(128, dtype=bfd), name="ident_b")
    ident_f = nc.inline_tensor(np.eye(128, dtype=np.float32), name="ident_f")
    ut_f = nc.inline_tensor(np.triu(np.ones((128, 128), dtype=np.float32)), name="ut_f")
    ut_b = nc.inline_tensor(np.triu(np.ones((128, 128))).astype(bfd), name="ut_b")
    ones65 = nc.inline_tensor(np.ones((65, 128), dtype=bfd), name="ones65_b")
    onescol = nc.inline_tensor(np.ones((128, 1), dtype=bfd), name="onescol_b")
    sel_np = np.zeros((E, E * 128), dtype=bfd)
    for e in range(E):
        sel_np[e, e * 128:(e + 1) * 128] = 1.0
    sel_h = nc.inline_tensor(sel_np, name="sel_b")

    with ExitStack() as ctx:
        En = ctx.enter_context
        xres = En(nc.sbuf_tensor("xres", [128, NT * D], F32))
        xnt = En(nc.sbuf_tensor("xnt", [128, DCH * TC], BF16))
        xn = En(nc.sbuf_tensor("xn", [128, D], F32))
        wup_b = [En(nc.sbuf_tensor(f"wupb{i}", [128, DCH * QH], BF16)) for i in range(2)]
        wdn_b = [En(nc.sbuf_tensor(f"wdnb{i}", [128, QHC * D], BF16)) for i in range(2)]
        hT = En(nc.sbuf_tensor("hT", [128, QHC * TGW], BF16))
        cwB = En(nc.sbuf_tensor("cwB", [128, TGW], BF16))
        scrb = En(nc.sbuf_tensor("scrb", [128, 2 * TGW], BF16))
        cwT = En(nc.sbuf_tensor("cwT", [E, TC], BF16))
        cwt = En(nc.sbuf_tensor("cwt", [128, NT * E], F32))
        gwC = En(nc.sbuf_tensor("gwC", [128, NT * E], F32))
        gsc = En(nc.sbuf_tensor("gsc", [128, 64], F32))
        stw = En(nc.sbuf_tensor("stw", [128, 64], F32))
        scr = En(nc.sbuf_tensor("scr", [128, max(512, (cores + 1) * HD, 2 * NT * E)], F32))
        bupT_s = En(nc.sbuf_tensor("bupTs", [128, 2 * HC_ALL], F32))
        bdn_s = En(nc.sbuf_tensor("bdns", [E, D], BF16))
        sel_s = En(nc.sbuf_tensor("sels", [E, E * 128], BF16))
        wg_s = En(nc.sbuf_tensor("wgs", [128, DCH * E], BF16))
        wq_s = En(nc.sbuf_tensor("wqs", [128, DCH * HD], BF16))
        wk_s = En(nc.sbuf_tensor("wks", [128, DCH * HD], BF16))
        wv_s = En(nc.sbuf_tensor("wvs", [128, DCH * HD], BF16))
        wo_s = En(nc.sbuf_tensor("wos", [HD, D], BF16))
        qTc = En(nc.sbuf_tensor("qTc", [HD, 2 * 128], BF16))
        vTc = En(nc.sbuf_tensor("vTc", [HD, 2 * 128], BF16))
        kn = En(nc.sbuf_tensor("kn", [128, NT * HD], BF16))
        qn = En(nc.sbuf_tensor("qn", [128, 2 * HD], F32))
        vn = En(nc.sbuf_tensor("vn", [128, 2 * HD], BF16))
        Bm = En(nc.sbuf_tensor("Bm", [128, 2 * 128], BF16))
        osb = En(nc.sbuf_tensor("osb", [128, 2 * HD], BF16))
        oT = En(nc.sbuf_tensor("oT", [HD, 2 * 128], BF16))
        carr = En(nc.sbuf_tensor("carr", [65, (NT + 1) * HD], F32))
        carrb = En(nc.sbuf_tensor("carrb", [65, 16 * HD], BF16))
        pin = En(nc.sbuf_tensor("pin", [65, HD], F32))
        rkm_s = En(nc.sbuf_tensor("rkms", [65, 8], F32))
        ident_bs = En(nc.sbuf_tensor("identbs", [128, 128], BF16))
        ident_fs = En(nc.sbuf_tensor("identfs", [128, 128], F32))
        ut_fs = En(nc.sbuf_tensor("utfs", [128, 128], F32))
        ut_bs = En(nc.sbuf_tensor("utbs", [128, 128], BF16))
        ones65_s = En(nc.sbuf_tensor("ones65s", [65, 128], BF16))
        onescol_s = En(nc.sbuf_tensor("onescols", [128, 1], BF16))
        tqz = En(nc.sbuf_tensor("tqz", [128, HD], F32))

        pU = [En(nc.psum_tensor(f"pU{i}", [128, 512], F32)) for i in range(2)]
        pDA = En(nc.psum_tensor("pDA", [128, 1024], F32))
        pDB = En(nc.psum_tensor("pDB", [128, 1024], F32))
        pW = En(nc.psum_tensor("pW", [128, 512], F32))
        pT = En(nc.psum_tensor("pT", [128, 128], BF16))
        pD = [pDA, pDB]

        b = Build(nc)
        for s in ("xd", "ld", "wd0", "wd1", "stg", "st", "cc", "ccd", "pe", "ve", "se"):
            b.sem(s)

        xr3 = x_h.ap().rearrange("(i p) d -> p i d", p=128)
        step = max(1, NT // 4)
        for j in range(0, NT, step):
            hi = min(NT, j + step)
            b.op("sync", "dma_start",
                 out=xres[:, :].rearrange("p (i d) -> p i d", d=D)[:, j:hi, :],
                 in_=xr3[:, j:hi, :], inc=("xd", 16))
        for dst, src in ((ident_bs, ident_b), (ident_fs, ident_f), (ut_fs, ut_f),
                         (ut_bs, ut_b), (ones65_s, ones65), (onescol_s, onescol),
                         (sel_s, sel_h), (rkm_s, rkm_h)):
            b.op("sync", "dma_start", out=dst[:, :], in_=src.ap(), inc=("xd", 16))
        XD_ALL = b.cnt["xd"]

        b.op("gpsimd", "dma_start",
             out=wg_s[:, :].rearrange("p (k g) -> p k g", g=E),
             in_=wg_h.ap().rearrange("(k p) g -> p k g", p=128), inc=("ld", 16))
        for (dst, src) in ((wq_s, wq_h), (wk_s, wk_h), (wv_s, wv_h)):
            b.op("gpsimd", "dma_start",
                 out=dst[:, :].rearrange("p (k h) -> p k h", h=HD),
                 in_=src.ap().rearrange("(k p) h -> p k h", p=128), inc=("ld", 16))
        b.op("gpsimd", "dma_start", out=wo_s[:, :], in_=wo_h.ap(), inc=("ld", 16))
        b.op("gpsimd", "dma_start", out=bdn_s[:, :], in_=bdn_h.ap(), inc=("ld", 16))
        LD_ALL = b.cnt["ld"]

        TR = {"evac": 0, "xn_free": {0: 0, 1: 0}, "ln_ev": {0: 0, 1: 0}}
        LNST = {"std": {0: 0, 1: 0}}
        GST = {"exp": {0: 0}, "tr": {0: 0}}
        GL = {"ev": 0}

        def gate_logits_cb(i, tile_ve):
            gwL = scr[:, 0:NT * E]
            if i == 0:
                b.wait("tensor", "ld", LD_ALL)
                b.wait("tensor", "ve", CWB["evac"])
            b.wait("tensor", "ve", GL["ev"])
            b.wait("tensor", "ve", tile_ve)
            mms = [(pW[:, 0:E], xnt[:, k * TC + i * 128: k * TC + (i + 1) * 128],
                    wg_s[:, k * E:(k + 1) * E], k == 0, k == DCH - 1) for k in range(DCH)]
            vmm = b.mmgroup(mms, inc=("pe", 1))
            b.wait("vector", "pe", vmm)
            if i == 0:
                b.wait("vector", "se", GST["exp"][0])
            GL["ev"] = b.op("vector", "tensor_copy", gwL[:, i * E:(i + 1) * E],
                            pW[0:128, 0:E], inc=("ve", 1))
        CWB = {"evac": 0}
        W = {"gq": 0, "qres": [], "bup_free": {0: 0, 1: 0}, "bup_v": 0}

        def gen_ln(sq_wait, xnt_busy_pe, scr_free_ve=0, pu_free_se=0, pu_free_ve=0, tile_cb=None):
            ve_done = 0
            NBN = (D + 511) // 512
            for i in range(NT):
                par = i % 2
                sb = par * 32
                xt = xres[:, i * D:(i + 1) * D]
                if i == 0:
                    b.wait("vector", "xd", XD_ALL)
                b.wait("vector", "se", LNST["std"][par])
                for j in range(NBN):
                    w0 = j * 512
                    w1 = min(D, w0 + 512)
                    b.op("vector", "bn_stats", stw[:, sb + 16 + 6 * j: sb + 22 + 6 * j],
                         xt[:, w0:w1])
                b.op("vector", "bn_aggr", stw[:, sb:sb + 2], stw[:, sb + 16: sb + 16 + 6 * NBN])
                vvar = b.op("vector", "tensor_scalar_add", stw[:, sb + 1:sb + 2],
                            stw[:, sb + 1:sb + 2], 1e-5, inc=("ve", 1))
                b.wait("scalar", "ve", vvar)
                vstd = b.op("scalar", "activation", stw[:, sb + 4:sb + 5], stw[:, sb + 1:sb + 2], AF.Sqrt, inc=("se", 1))
                LNST["std"][par] = vstd
                b.wait("vector", "se", vstd)
                b.op("vector", "reciprocal", stw[:, sb + 5:sb + 6], stw[:, sb + 4:sb + 5])
                b.wait("vector", "pe", TR["xn_free"][0])
                vno = b.op("vector", "tensor_scalar", xn[:, 0:D], xt,
                           stw[:, sb:sb + 1], stw[:, sb + 5:sb + 6],
                           ALU.subtract, ALU.mult, inc=("ve", 1))
                b.wait("tensor", "ve", vno)
                if i == 0:
                    b.wait("tensor", "xd", XD_ALL)
                    b.wait("tensor", "se", pu_free_se)
                    b.wait("tensor", "ve", pu_free_ve)
                for k in range(DCH):
                    bk2 = k % 2
                    b.wait("tensor", "ve", TR["ln_ev"][bk2])
                    vtr = b.op("tensor", "transpose", pU[bk2][:, 0:128],
                               xn[:, k * 128:(k + 1) * 128],
                               ident_fs[:, :], inc=("pe", 1))
                    if k == DCH - 1:
                        TR["xn_free"][0] = vtr
                    b.wait("vector", "pe", vtr)
                    if i == 0 and k == 0:
                        b.wait("vector", "pe", xnt_busy_pe)
                    TR["ln_ev"][bk2] = b.op("vector", "tensor_copy",
                                            xnt[:, k * TC + i * 128: k * TC + (i + 1) * 128],
                                            pU[bk2][0:128, 0:128], inc=("ve", 1))
                ve_done = TR["ln_ev"][(DCH - 1) % 2]
                TR["evac"] = max(TR["ln_ev"][0], TR["ln_ev"][1])
                if tile_cb is not None:
                    tile_cb(i, max(TR["ln_ev"][0], TR["ln_ev"][1]))
            return max(TR["ln_ev"][0], TR["ln_ev"][1])

        def gen_gate(xnt_ve):
            GE = NT * E
            gwL = scr[:, 0:GE]
            gwB = scr[:, GE:2 * GE]
            m1 = gsc[:, 0:NT]
            m2r = gsc[:, 16:16 + NT]
            m3r = gsc[:, 32:32 + NT]
            rsg = gsc[:, 48:48 + NT]

            def seg(ap):
                return ap.rearrange("p (g e) -> p g e", e=E)

            def bc(ap_col):
                base = ap_col
                return bass.AP(base.tensor, base.offset,
                               [[base.ap[0][0], 128], [1, NT], [0, E]])

            b.op("vector", "tensor_reduce", m1[:, :].rearrange("p (g o) -> p g o", o=1),
                 seg(gwL), axis=AX.X, op=ALU.max)
            b.op("vector", "tensor_tensor", seg(gwC[:, :]), seg(gwL), bc(m1), ALU.is_ge)
            b.op("vector", "scalar_tensor_tensor", gwB, gwC[:, :], -1e30, gwL[:, 0:GE],
                 ALU.mult, ALU.add)
            b.op("vector", "tensor_reduce", m2r[:, :].rearrange("p (g o) -> p g o", o=1),
                 seg(gwB), axis=AX.X, op=ALU.max)
            b.op("vector", "tensor_tensor", seg(gwC[:, :]), seg(gwB), bc(m2r), ALU.is_ge)
            b.op("vector", "scalar_tensor_tensor", gwB, gwC[:, :], -1e30, gwB,
                 ALU.mult, ALU.add)
            b.op("vector", "tensor_reduce", m3r[:, :].rearrange("p (g o) -> p g o", o=1),
                 seg(gwB), axis=AX.X, op=ALU.max)
            vmask = b.op("vector", "tensor_tensor", seg(gwC[:, :]), seg(gwL), bc(m3r),
                         ALU.is_ge, inc=("ve", 1))
            vlc = b.op("vector", "tensor_tensor", seg(gwL), seg(gwL), bc(m1), ALU.subtract)
            b.wait("scalar", "ve", vlc)
            vexp = b.op("scalar", "activation", gwB, gwL[:, 0:GE], AF.Exp, inc=("se", 1))
            GST["exp"][0] = vexp
            b.wait("vector", "se", vexp)
            b.op("vector", "tensor_mul", gwB, gwB, gwC[:, :])
            b.op("vector", "tensor_reduce", rsg[:, :].rearrange("p (g o) -> p g o", o=1),
                 seg(gwB), axis=AX.X, op=ALU.add)
            b.op("vector", "reciprocal", rsg[:, :], rsg[:, :])
            b.op("vector", "tensor_tensor", seg(gwB), seg(gwB), bc(rsg), ALU.mult)
            b.wait("vector", "pe", GST["tr"][0])
            vcw = b.op("vector", "tensor_scalar_mul", cwt[:, 0:GE], gwB, 0.5, inc=("ve", 1))
            prev_evac = 0
            ev = [0, 0]
            for i in range(NT):
                bk = i % 2
                b.wait("tensor", "ve", vcw if i == 0 else 0)
                if ev[bk]:
                    b.wait("tensor", "ve", ev[bk])
                vtr = b.op("tensor", "transpose", pU[bk][0:E, 0:128],
                           cwt[:, i * E:(i + 1) * E], ident_fs[:, :], inc=("pe", 1))
                GST["tr"][0] = vtr
                b.wait("vector", "pe", vtr)
                ev[bk] = b.op("vector", "tensor_copy", cwT[0:E, i * 128:(i + 1) * 128],
                              pU[bk][0:E, 0:128], inc=("ve", 1))
                prev_evac = ev[bk]
            return max(ev[0], ev[1])

        def emit_wdma(e, q):
            gq = W["gq"]
            par = gq % 2
            if q == 0:
                ep = (W["gq"] // NQ) % 2
                if W["bup_free"][ep]:
                    b.wait("scalar", "se", W["bup_free"][ep])
                b.op("scalar", "dma_start", out=bupT_s[:, ep * HC_ALL:(ep + 1) * HC_ALL],
                     in_=bupT_h.ap()[e], inc=("stg", 16))
                W["bup_v"] = b.cnt["stg"]
            if gq >= 2:
                b.wait("gpsimd", "ve", W["qres"][gq - 2])
            wdsem = f"wd{par}"
            b.op("gpsimd", "dma_start",
                 out=wup_b[par][:, :].rearrange("p (k h) -> p k h", h=QH),
                 in_=wup_h.ap()[e, :, q * QH:(q + 1) * QH].rearrange("(k p) h -> p k h", p=128),
                 inc=(wdsem, 16))
            b.op("gpsimd", "dma_start",
                 out=wdn_b[par][:, :].rearrange("p (c d) -> p c d", d=D),
                 in_=wdn_h.ap()[e, q * QH:(q + 1) * QH, :].rearrange("(c p) d -> p c d", p=128),
                 inc=(wdsem, 16))
            W["gq"] += 1
            return gq

        def gen_ffn(xnt_ve, cw_ve, scr_free_ve, pre_waits):
            gelu_hist, cwm_hist, res_hist = [], [], []
            slice_res = [0] * NT
            dn_cnt = 0
            first = True
            dnlast = 0
            for e in range(E):
                for q in range(NQ):
                    gq = emit_wdma(e, q)
                    par = gq % 2
                    b.wait("tensor", f"wd{gq % 2}", 32 * (gq // 2 + 1))
                    if first:
                        b.wait("tensor", "ve", cw_ve)
                        b.wait("tensor", "ve", xnt_ve)
                        for (s_, v_) in pre_waits:
                            b.wait("tensor", s_, v_)
                        first = False
                    for tg in range(NTG):
                        b.wait("tensor", "ve", CWB["evac"])
                        vcwb = b.op("tensor", "matmul", pW[:, 0:TGW],
                                    sel_s[:, e * 128:(e + 1) * 128],
                                    cwT[0:E, tg * TGW:(tg + 1) * TGW],
                                    start=True, stop=True, inc=("pe", 1))
                        b.wait("vector", "pe", vcwb)
                        CWB["evac"] = b.op("vector", "tensor_copy", cwB[:, 0:TGW], pW[:, 0:TGW],
                                           inc=("ve", 1))
                        for hc in range(QHC):
                            g = len(gelu_hist)
                            bk = g % 2
                            if g >= 2:
                                b.wait("tensor", "se", gelu_hist[-2])
                            mms = [(pU[bk][:, 0:TGW],
                                    wup_b[par][:, k * QH + hc * 128: k * QH + (hc + 1) * 128],
                                    xnt[:, k * TC + tg * TGW: k * TC + (tg + 1) * TGW],
                                    k == 0, k == DCH - 1) for k in range(DCH)]
                            vup = b.mmgroup(mms, inc=("pe", 1))
                            b.wait("scalar", "pe", vup)
                            b.wait("scalar", "ve",
                                   cwm_hist[-2] if len(cwm_hist) >= 2 else scr_free_ve)
                            if q == 0 and tg == 0 and hc == 0:
                                b.wait("scalar", "stg", W["bup_v"])
                            ep = e % 2
                            bias = bupT_s[:, ep * HC_ALL + q * QHC + hc: ep * HC_ALL + q * QHC + hc + 1]
                            vg = b.op("scalar", "activation", scrb[:, bk * TGW:(bk + 1) * TGW],
                                      pU[bk][:, 0:TGW], AF.Gelu, bias=bias, inc=("se", 1))
                            gelu_hist.append(vg)
                            if q == NQ - 1 and tg == NTG - 1 and hc == QHC - 1:
                                W["bup_free"][ep] = vg
                            b.wait("vector", "se", vg)
                            b.wait("vector", "pe", dnlast)
                            vcm = b.op("vector", "tensor_mul", hT[:, hc * TGW:(hc + 1) * TGW],
                                       scrb[:, bk * TGW:(bk + 1) * TGW], cwB[:, 0:TGW],
                                       inc=("ve", 1))
                            cwm_hist.append(vcm)
                        tg_cwm = cwm_hist[-QHC:]
                        for sl in range(NSL):
                            tile = tg * NSL + sl
                            bk = dn_cnt % 2
                            mms = []
                            if e == 0 and q == 0:
                                for nh in range(NNH):
                                    mms.append((pD[bk][:, nh * NW:(nh + 1) * NW],
                                                cwT[0:E, tile * 128:(tile + 1) * 128],
                                                bdn_s[0:E, nh * NW:(nh + 1) * NW], True, False))
                            for c in range(QHC):
                                if sl == 0:
                                    mms.append(("wait", "ve", tg_cwm[c]))
                                st_ = (c == 0) and not (e == 0 and q == 0)
                                for nh in range(NNH):
                                    mms.append((pD[bk][:, nh * NW:(nh + 1) * NW],
                                                hT[:, c * TGW + sl * 128: c * TGW + (sl + 1) * 128],
                                                wdn_b[par][:, c * D + nh * NW: c * D + (nh + 1) * NW],
                                                st_, c == QHC - 1))
                            if sl > 0:
                                b.wait("tensor", "ve", tg_cwm[-1])
                            if len(res_hist) >= 2:
                                b.wait("tensor", "ve", res_hist[-2])
                            vdn = b.mmgroup(mms, inc=("pe", 1))
                            dnlast = vdn
                            dn_cnt += 1
                            b.wait("vector", "pe", vdn)
                            xt = xres[:, tile * D:(tile + 1) * D]
                            vres = b.op("vector", "tensor_add", xt, pD[bk][:, 0:D], xt, inc=("ve", 1))
                            res_hist.append(vres)
                            slice_res[tile] = vres
                    W["qres"].append(res_hist[-1])
            return slice_res, gelu_hist[-1], cwm_hist[-1]

        ln1_ve = gen_ln([0] * NT, 0, tile_cb=gate_logits_cb)
        cw1_ve = gen_gate(ln1_ve)
        CWB["evac"] = max(CWB["evac"], GL["ev"])
        res1, gelu1_se, cwm1_ve = gen_ffn(ln1_ve, cw1_ve, 0, [])
        final_res = res1

        if stage >= 2:
            pe_f1 = b.cnt["pe"]
            maxres1 = max(res1)
            ln2_ve = gen_ln(res1, pe_f1, cwm1_ve, pu_free_se=gelu1_se)

            b.op("vector", "memset", carr[:, 0:HD], 0.0)
            pack_ve = 0
            vn_ev = 0
            kn_ve = [0] * NT
            knf = {0: 0, 1: 0}
            vnf = {0: maxres1, 1: maxres1}
            vn_evs = [0] * NT
            for c in range(NT):
                bk = c % 2
                if c == 0:
                    b.wait("tensor", "ve", ln2_ve)
                    b.wait("tensor", "se", gelu1_se)
                b.wait("tensor", "ve", knf[bk])
                mms = [(pU[bk][:, 0:HD], xnt[:, k * TC + c * 128: k * TC + (c + 1) * 128],
                        wk_s[:, k * HD:(k + 1) * HD], k == 0, k == DCH - 1) for k in range(DCH)]
                vk = b.mmgroup(mms, inc=("pe", 1))
                b.wait("tensor", "ve", vnf[bk])
                mms = [(pD[bk][:, 0:HD], xnt[:, k * TC + c * 128: k * TC + (c + 1) * 128],
                        wv_s[:, k * HD:(k + 1) * HD], k == 0, k == DCH - 1) for k in range(DCH)]
                vv = b.mmgroup(mms, inc=("pe", 1))
                b.wait("vector", "pe", vk)
                b.op("vector", "tensor_scalar_min", scr[:, 0:HD], pU[bk][:, 0:HD], 0.0)
                vmx = b.op("vector", "tensor_scalar_max", scr[:, HD:2 * HD], pU[bk][:, 0:HD], 0.0,
                           inc=("ve", 1))
                knf[bk] = vmx
                b.wait("scalar", "ve", vmx)
                vex = b.op("scalar", "activation", scr[:, 0:HD], scr[:, 0:HD], AF.Exp, inc=("se", 1))
                b.wait("vector", "se", vex)
                kn_ve[c] = b.op("vector", "tensor_add", kn[:, c * HD:(c + 1) * HD],
                                scr[:, 0:HD], scr[:, HD:2 * HD], inc=("ve", 1))
                b.wait("vector", "pe", vv)
                vn_ev = b.op("vector", "tensor_copy", vn[:, bk * HD:(bk + 1) * HD],
                             pD[bk][:, 0:HD], inc=("ve", 1))
                vnf[bk] = vn_ev
                vn_evs[c] = vn_ev
                b.wait("tensor", "ve", vn_ev)
                b.wait("tensor", "ve", pack_ve if pack_ve else CWB["evac"])
                mms = [(pW[0:HD, 0:HD], vn[:, bk * HD:(bk + 1) * HD],
                        kn[:, c * HD:(c + 1) * HD], True, True),
                       (pW[64:65, 0:HD], onescol_s[:, :], kn[:, c * HD:(c + 1) * HD], True, True)]
                vpk = b.mmgroup(mms, inc=("pe", 1))
                b.wait("vector", "pe", vpk)
                pack_ve = b.op("vector", "tensor_add", carr[:, (c + 1) * HD:(c + 2) * HD],
                               pW[0:65, 0:HD], carr[:, c * HD:(c + 1) * HD], inc=("ve", 1))
            CWB["evac"] = pack_ve

            b.wait("scalar", "ve", pack_ve)
            b.op("scalar", "dma_start", out=cc_in.ap(), in_=carr[:, NT * HD:(NT + 1) * HD],
                 inc=("stg", 16))
            b.wait("gpsimd", "stg", b.cnt["stg"])
            vcc = b.op("gpsimd", "collective_compute", "AllGather", ALU.bypass,
                       replica_groups=[list(range(cores))],
                       ins=[cc_in.ap().opt()], outs=[cc_out.ap().opt()], inc=("cc", 1))
            b.wait("gpsimd", "cc", vcc)
            b.wait("gpsimd", "ve", pack_ve)
            b.op("gpsimd", "dma_start",
                 out=scr[0:65, 0:cores * HD].rearrange("p (r h) -> p r h", h=HD),
                 in_=cc_out.ap().rearrange("(r p) h -> p r h", p=65), inc=("ccd", 16))
            b.wait("vector", "ccd", b.cnt["ccd"])
            b.op("vector", "memset", pin[:, :], 0.0)
            for r in range(cores):
                b.op("vector", "scalar_tensor_tensor", pin[0:65, :],
                     scr[0:65, r * HD:(r + 1) * HD], rkm_s[:, r:r + 1], pin[0:65, :],
                     ALU.mult, ALU.add)
            pin_b = bass.AP(pin[:, :].tensor, pin[:, :].offset,
                            [[pin[:, :].ap[0][0], 65], [0, NT], [1, HD]])
            vcb_all = b.op("vector", "tensor_tensor",
                           carrb[:, 0:NT * HD].rearrange("p (c h) -> p c h", h=HD),
                           carr[:, 0:NT * HD].rearrange("p (c h) -> p c h", h=HD),
                           pin_b, ALU.add, inc=("ve", 1))

            o_se = 0
            den_ve = 0
            ao_res = 0
            res2 = [0] * NT
            vmsk_arr = [0] * NT
            prev_vtc = 0
            prev_vmsk = 0
            otr = {0: 0, 1: 0}
            attn_scr_last = 0

            def stageA(c):
                nonlocal prev_vtc, prev_vmsk, attn_scr_last
                p2 = c % 2
                b.wait("tensor", "ve", prev_vtc if prev_vtc else vn_ev)
                mms = [(pU[0][0:HD, 0:128], wq_s[:, k * HD:(k + 1) * HD],
                        xnt[:, k * TC + c * 128: k * TC + (c + 1) * 128],
                        k == 0, k == DCH - 1) for k in range(DCH)]
                vqt = b.mmgroup(mms, inc=("pe", 1))
                b.wait("vector", "pe", vqt)
                b.op("vector", "tensor_scalar_min", scr[0:HD, 0:128], pU[0][0:HD, 0:128], 0.0)
                vmx = b.op("vector", "tensor_scalar_max", scr[0:HD, 128:256],
                           pU[0][0:HD, 0:128], 0.0, inc=("ve", 1))
                b.wait("scalar", "ve", vmx)
                vex = b.op("scalar", "activation", scr[0:HD, 0:128], scr[0:HD, 0:128],
                           AF.Exp, inc=("se", 1))
                b.wait("vector", "se", vex)
                vq_add = b.op("vector", "tensor_add", qTc[0:HD, p2 * 128:(p2 + 1) * 128],
                              scr[0:HD, 0:128], scr[0:HD, 128:256], inc=("ve", 1))
                b.wait("tensor", "ve", vq_add)
                b.wait("tensor", "ve", TR["evac"])
                vtrq = b.op("tensor", "transpose", pT[:, 0:HD],
                            qTc[0:HD, p2 * 128:(p2 + 1) * 128], ident_bs[0:HD, 0:HD],
                            inc=("pe", 1))
                b.wait("vector", "pe", vtrq)
                TR["evac"] = b.op("vector", "tensor_copy", qn[:, p2 * HD:(p2 + 1) * HD],
                                  pT[:, 0:HD], inc=("ve", 1))
                attn_scr_last = vq_add
                b.wait("tensor", "ve", vq_add)
                mms = [(pU[0][0:HD, 0:128], wv_s[:, k * HD:(k + 1) * HD],
                        xnt[:, k * TC + c * 128: k * TC + (c + 1) * 128],
                        k == 0, k == DCH - 1) for k in range(DCH)]
                vvt = b.mmgroup(mms, inc=("pe", 1))
                b.wait("vector", "pe", vvt)
                prev_vtc = b.op("vector", "tensor_copy", vTc[0:HD, p2 * 128:(p2 + 1) * 128],
                                pU[0][0:HD, 0:128], inc=("ve", 1))
                b.wait("tensor", "ve", prev_vtc)
                vB = b.op("tensor", "matmul", pU[1][:, 0:128],
                          vTc[0:HD, p2 * 128:(p2 + 1) * 128],
                          qTc[0:HD, p2 * 128:(p2 + 1) * 128],
                          start=True, stop=True, inc=("pe", 1))
                b.wait("vector", "pe", vB)
                prev_vmsk = b.op("vector", "tensor_mul", Bm[:, p2 * 128:(p2 + 1) * 128],
                                 pU[1][:, 0:128], ut_fs[:, :], inc=("ve", 1))
                vmsk_arr[c] = prev_vmsk

            def stageB(c):
                nonlocal o_se, den_ve, ao_res
                p2 = c % 2
                b.wait("tensor", "ve", vcb_all)
                b.wait("tensor", "se", o_se)
                if c < 2:
                    b.wait("tensor", "ve", max(maxres1, vn_evs[NT - 2 + c]))
                mms = [(pDA[:, 0:HD], Bm[:, p2 * 128:(p2 + 1) * 128],
                        kn[:, c * HD:(c + 1) * HD], True, False),
                       (pDA[:, 0:HD], qTc[0:HD, p2 * 128:(p2 + 1) * 128],
                        carrb[0:HD, c * HD:(c + 1) * HD], False, True)]
                vO = b.mmgroup(mms, inc=("pe", 1))
                b.wait("tensor", "ve", den_ve)
                mms = [(pDA[:, 512:512 + HD], ut_bs[:, :], kn[:, c * HD:(c + 1) * HD], True, False),
                       (pDA[:, 512:512 + HD], ones65_s[64:65, 0:128],
                        carrb[64:65, c * HD:(c + 1) * HD], False, True)]
                vZ = b.mmgroup(mms, inc=("pe", 1))
                b.wait("vector", "pe", vZ)
                b.wait("vector", "se", o_se)
                b.op("vector", "tensor_mul", tqz[:, 0:HD], pDA[:, 512:512 + HD],
                     qn[:, p2 * HD:(p2 + 1) * HD])
                b.op("vector", "reduce_sum", stw[:, 14:15], tqz[:, 0:HD], axis=AX.X)
                b.op("vector", "tensor_scalar_max", stw[:, 14:15], stw[:, 14:15], 1e-6)
                den_ve = b.op("vector", "reciprocal", stw[:, 15:16], stw[:, 14:15], inc=("ve", 1))
                b.wait("scalar", "ve", den_ve)
                b.wait("scalar", "pe", vO)
                b.wait("scalar", "pe", otr[p2])
                o_se = b.op("scalar", "activation", osb[:, p2 * HD:(p2 + 1) * HD], pDA[:, 0:HD],
                            AF.Copy, scale=stw[:, 15:16], inc=("se", 1))
                b.wait("tensor", "se", o_se)
                b.wait("tensor", "ve", TR["evac"])
                vtr = b.op("tensor", "transpose", pT[0:HD, 0:128], osb[:, p2 * HD:(p2 + 1) * HD],
                           ident_bs[:, :], inc=("pe", 1))
                otr[p2] = vtr
                b.wait("vector", "pe", vtr)
                TR["evac"] = b.op("vector", "tensor_copy", oT[0:HD, p2 * 128:(p2 + 1) * 128],
                                  pT[0:HD, 0:128], inc=("ve", 1))
                b.wait("tensor", "ve", TR["evac"])
                b.wait("tensor", "ve", ao_res if ao_res else max(maxres1, vn_evs[NT - 1]))
                mms = [(pDB[:, nh * NW:(nh + 1) * NW], oT[0:HD, p2 * 128:(p2 + 1) * 128],
                        wo_s[0:HD, nh * NW:(nh + 1) * NW], True, True) for nh in range(NNH)]
                vAO = b.mmgroup(mms, inc=("pe", 1))
                b.wait("vector", "pe", vAO)
                xt = xres[:, c * D:(c + 1) * D]
                ao_res = b.op("vector", "tensor_add", xt, pDB[:, 0:D], xt, inc=("ve", 1))
                res2[c] = ao_res

            stageA(0)
            for c in range(NT):
                if c + 1 < NT:
                    stageA(c + 1)
                stageB(c)
            final_res = res2


        if stage >= 3:
            CWB["evac"] = max(CWB["evac"], pack_ve)
            ln3_ve = gen_ln(res2, b.cnt["pe"], attn_scr_last, pu_free_ve=max(res2),
                            tile_cb=gate_logits_cb)
            cw3_ve = gen_gate(ln3_ve)
            pre = [("se", o_se), ("ve", ao_res), ("ve", den_ve)]
            CWB["evac"] = max(CWB["evac"], GL["ev"])
            res3, _, _ = gen_ffn(ln3_ve, cw3_ve, attn_scr_last, pre)
            final_res = res3

        b.wait("sync", "ve", max(final_res))
        yr3 = y_h.ap().rearrange("(i p) d -> p i d", p=128)
        for j in range(0, NT, step):
            hi = min(NT, j + step)
            b.op("sync", "dma_start",
                 out=yr3[:, j:hi, :],
                 in_=xres[:, :].rearrange("p (i d) -> p i d", d=D)[:, j:hi, :], inc=("st", 16))
        b.wait("sync", "st", b.cnt["st"])

        import os as _os
        if _os.environ.get("KBUILD_DEBUG"):
            print("SBUF remaining:", nc.sbuf_bytes_remaining, "PSUM banks left:", nc.psum_banks_remaining)
        with nc.Block() as block:
            @block.tensor
            def _(t):
                for f in b.P["tensor"]:
                    f(t)

            @block.vector
            def _(v):
                for f in b.P["vector"]:
                    f(v)

            @block.scalar
            def _(s):
                for f in b.P["scalar"]:
                    f(s)

            @block.gpsimd
            def _(g):
                for f in b.P["gpsimd"]:
                    f(g)

            @block.sync
            def _(sy):
                for f in b.P["sync"]:
                    f(sy)

    return nc


_build_cache = {}


def get_nc(cfg=None, n_cores=None, stage=3):
    key = (tuple(sorted((cfg or REAL_CFG).items())), n_cores, stage)
    if key not in _build_cache:
        _build_cache[key] = build_nc(dict(cfg or REAL_CFG), n_cores, stage)
    return _build_cache[key]


def make_in_maps(inputs, cfg=None, n_cores=8):
    cfg = dict(cfg or REAL_CFG)
    TC, D, E, HID = cfg["TC"], cfg["D"], cfg["E"], cfg["HID"]
    x = np.ascontiguousarray(np.asarray(inputs["L0_attn_x"], dtype=np.float32).reshape(-1, D))
    bup = np.asarray(inputs["L0_ffn_bup"], dtype=np.float32)
    bupT = np.ascontiguousarray(bup.reshape(E, HID // 128, 128).transpose(0, 2, 1))
    common = dict(
        wq=np.ascontiguousarray(np.asarray(inputs["L0_attn_W_q"], dtype=np.float32)),
        wk=np.ascontiguousarray(np.asarray(inputs["L0_attn_W_k"], dtype=np.float32)),
        wv=np.ascontiguousarray(np.asarray(inputs["L0_attn_W_v"], dtype=np.float32)),
        wo=np.ascontiguousarray(np.asarray(inputs["L0_attn_W_o"], dtype=np.float32)),
        wg=np.ascontiguousarray(np.asarray(inputs["L0_ffn_Wg"], dtype=np.float32)),
        wup=np.ascontiguousarray(np.asarray(inputs["L0_ffn_Wup"], dtype=np.float32)),
        bupT=bupT,
        wdn=np.ascontiguousarray(np.asarray(inputs["L0_ffn_Wdown"], dtype=np.float32)),
        bdn=np.ascontiguousarray(np.asarray(inputs["L0_ffn_bdown"], dtype=np.float32)),
    )
    in_maps = []
    for c in range(n_cores):
        rkm = np.zeros((65, 8), dtype=np.float32)
        rkm[:, :c] = 1.0
        m = dict(common)
        m["x"] = np.ascontiguousarray(x[c * TC:(c + 1) * TC])
        m["rkm"] = rkm
        in_maps.append(m)
    return in_maps


def kernel(**inputs):
    nc = get_nc()
    in_maps = make_in_maps(inputs)
    res = run_bass_kernel_spmd(nc, in_maps, core_ids=list(range(8)))
    out = np.concatenate([r["y"] for r in res.results], axis=0)
    return out.reshape(1, 16384, 1024)


# version 08
# speedup vs baseline: 1.0515x; 1.0482x over previous
import numpy as np
import ml_dtypes
from contextlib import ExitStack

import concourse.bass as bass
import concourse.mybir as mybir
from concourse.bass_utils import run_bass_kernel_spmd

F32 = mybir.dt.float32
BF16 = mybir.dt.bfloat16
AF = mybir.ActivationFunctionType
ALU = mybir.AluOpType
AX = mybir.AxisListType

REAL_CFG = dict(cores=8, TC=2048, D=1024, E=12, HID=4096, NQ=4, HD=64, TGW=512)
ENGINES = ("tensor", "vector", "scalar", "gpsimd", "sync")


class Build:
    def __init__(self, nc):
        self.nc = nc
        self.P = {e: [] for e in ENGINES}
        self.S = {}
        self.cnt = {}

    def sem(self, name):
        self.S[name] = self.nc.alloc_semaphore(name)
        self.cnt[name] = 0

    def bump(self, name, n=1):
        self.cnt[name] += n
        return self.cnt[name]

    def wait(self, eng, sem, val):
        if not val or val <= 0:
            return
        h = self.S[sem]
        self.P[eng].append(lambda E, h=h, v=val: E.wait_ge(h, v))

    AUTO = {"vector": "ve", "scalar": "se"}

    def op(self, eng, meth, *a, inc=None, **kw):
        auto = self.AUTO.get(eng) if meth not in ("dma_start", "collective_compute") else None
        if auto is not None:
            prev = self.cnt[auto]
            if prev:
                self.wait(eng, auto, prev)
            if inc is None:
                inc = (auto, 1)
        v = self.bump(*inc) if inc else None
        h = self.S[inc[0]] if inc else None
        n = inc[1] if inc else None

        def f(E, meth=meth, a=a, kw=kw, h=h, n=n):
            r = getattr(E, meth)(*a, **kw)
            if h is not None:
                r.then_inc(h, n)
        self.P[eng].append(f)
        return v

    def mmgroup(self, mms, inc=None):
        v = self.bump(*inc) if inc else None
        h = self.S[inc[0]] if inc else None
        n = inc[1] if inc else None
        mms = [(("wait", self.S[m[1]], m[2]) if m[0] == "wait" else m) for m in mms]

        def f(T, mms=mms, h=h, n=n):
            ins = None
            for m in mms:
                if m[0] == "wait":
                    if m[2] > 0:
                        T.wait_ge(m[1], m[2])
                    continue
                (o, l, r, st, sp) = m
                ins = T.matmul(o, l, r, start=st, stop=sp)
            if h is not None:
                ins.then_inc(h, n)
        self.P["tensor"].append(f)
        return v


def build_nc(cfg=None, n_cores=None, stage=3):
    cfg = dict(cfg or REAL_CFG)
    cores = n_cores if n_cores is not None else cfg["cores"]
    TC, D, E, HID, NQ, HD, TGW = (cfg[k] for k in ("TC", "D", "E", "HID", "NQ", "HD", "TGW"))
    NT = TC // 128
    NTG = TC // TGW
    NSL = TGW // 128
    DCH = D // 128
    QH = HID // NQ
    QHC = QH // 128
    HC_ALL = HID // 128
    NNH = max(1, D // 512)
    NW = D // NNH

    nc = bass.Bass("TRN2", target_bir_lowering=False)

    x_h = nc.dram_tensor("x", [TC, D], F32, kind="ExternalInput")
    wq_h = nc.dram_tensor("wq", [D, HD], F32, kind="ExternalInput")
    wk_h = nc.dram_tensor("wk", [D, HD], F32, kind="ExternalInput")
    wv_h = nc.dram_tensor("wv", [D, HD], F32, kind="ExternalInput")
    wo_h = nc.dram_tensor("wo", [HD, D], F32, kind="ExternalInput")
    wg_h = nc.dram_tensor("wg", [D, E], F32, kind="ExternalInput")
    wup_h = nc.dram_tensor("wup", [E, D, HID], F32, kind="ExternalInput")
    bupT_h = nc.dram_tensor("bupT", [E, 128, HID // 128], F32, kind="ExternalInput")
    wdn_h = nc.dram_tensor("wdn", [E, HID, D], F32, kind="ExternalInput")
    bdn_h = nc.dram_tensor("bdn", [E, D], F32, kind="ExternalInput")
    rkm_h = nc.dram_tensor("rkm", [65, 8], F32, kind="ExternalInput")
    y_h = nc.dram_tensor("y", [TC, D], F32, kind="ExternalOutput")

    cc_in = nc.dram_tensor("cc_in", [65, HD], F32, kind="Internal")
    cc_out = nc.dram_tensor("cc_out", [65 * cores, HD], F32, kind="Internal")

    bfd = ml_dtypes.bfloat16
    ident_b = nc.inline_tensor(np.eye(128, dtype=bfd), name="ident_b")
    ident_f = nc.inline_tensor(np.eye(128, dtype=np.float32), name="ident_f")
    ut_f = nc.inline_tensor(np.triu(np.ones((128, 128), dtype=np.float32)), name="ut_f")
    ut_b = nc.inline_tensor(np.triu(np.ones((128, 128))).astype(bfd), name="ut_b")
    ones65 = nc.inline_tensor(np.ones((65, 128), dtype=bfd), name="ones65_b")
    onescol = nc.inline_tensor(np.ones((128, 1), dtype=bfd), name="onescol_b")
    sel_np = np.zeros((E, E * 128), dtype=bfd)
    for e in range(E):
        sel_np[e, e * 128:(e + 1) * 128] = 1.0
    sel_h = nc.inline_tensor(sel_np, name="sel_b")

    with ExitStack() as ctx:
        En = ctx.enter_context
        xres = En(nc.sbuf_tensor("xres", [128, NT * D], F32))
        xnt = En(nc.sbuf_tensor("xnt", [128, DCH * TC], BF16))
        xn = En(nc.sbuf_tensor("xn", [128, D], F32))
        wup_b = [En(nc.sbuf_tensor(f"wupb{i}", [128, DCH * QH], BF16)) for i in range(2)]
        wdn_b = [En(nc.sbuf_tensor(f"wdnb{i}", [128, QHC * D], BF16)) for i in range(2)]
        hT = En(nc.sbuf_tensor("hT", [128, QHC * TGW], BF16))
        cwB = En(nc.sbuf_tensor("cwB", [128, TGW], BF16))
        scrb = En(nc.sbuf_tensor("scrb", [128, 2 * TGW], BF16))
        cwT = En(nc.sbuf_tensor("cwT", [E, TC], BF16))
        cwt = En(nc.sbuf_tensor("cwt", [128, NT * E], F32))
        gwC = En(nc.sbuf_tensor("gwC", [128, NT * E], F32))
        gsc = En(nc.sbuf_tensor("gsc", [128, 64], F32))
        lnmv = En(nc.sbuf_tensor("lnmv", [128, 2 * NT], F32))
        lnr = En(nc.sbuf_tensor("lnr", [128, NT], F32))
        stw = En(nc.sbuf_tensor("stw", [128, 64], F32))
        scr = En(nc.sbuf_tensor("scr", [128, max(512, (cores + 1) * HD, 2 * NT * E)], F32))
        bupT_s = En(nc.sbuf_tensor("bupTs", [128, 2 * HC_ALL], F32))
        bdn_s = En(nc.sbuf_tensor("bdns", [E, D], BF16))
        sel_s = En(nc.sbuf_tensor("sels", [E, E * 128], BF16))
        wg_s = En(nc.sbuf_tensor("wgs", [128, DCH * E], BF16))
        wq_s = En(nc.sbuf_tensor("wqs", [128, DCH * HD], BF16))
        wk_s = En(nc.sbuf_tensor("wks", [128, DCH * HD], BF16))
        wv_s = En(nc.sbuf_tensor("wvs", [128, DCH * HD], BF16))
        wo_s = En(nc.sbuf_tensor("wos", [HD, D], BF16))
        qTc = En(nc.sbuf_tensor("qTc", [HD, 2 * 128], BF16))
        vTc = En(nc.sbuf_tensor("vTc", [HD, 2 * 128], BF16))
        kn = En(nc.sbuf_tensor("kn", [128, NT * HD], BF16))
        qn = En(nc.sbuf_tensor("qn", [128, 2 * HD], F32))
        vn = En(nc.sbuf_tensor("vn", [128, 2 * HD], BF16))
        Bm = En(nc.sbuf_tensor("Bm", [128, 2 * 128], BF16))
        osb = En(nc.sbuf_tensor("osb", [128, 2 * HD], BF16))
        oT = En(nc.sbuf_tensor("oT", [HD, 2 * 128], BF16))
        carr = En(nc.sbuf_tensor("carr", [65, (NT + 1) * HD], F32))
        carrb = En(nc.sbuf_tensor("carrb", [65, 16 * HD], BF16))
        pin = En(nc.sbuf_tensor("pin", [65, HD], F32))
        rkm_s = En(nc.sbuf_tensor("rkms", [65, 8], F32))
        ident_bs = En(nc.sbuf_tensor("identbs", [128, 128], BF16))
        ident_fs = En(nc.sbuf_tensor("identfs", [128, 128], F32))
        ut_fs = En(nc.sbuf_tensor("utfs", [128, 128], F32))
        ut_bs = En(nc.sbuf_tensor("utbs", [128, 128], BF16))
        ones65_s = En(nc.sbuf_tensor("ones65s", [65, 128], BF16))
        onescol_s = En(nc.sbuf_tensor("onescols", [128, 1], BF16))
        tqz = En(nc.sbuf_tensor("tqz", [128, HD], F32))

        pU = [En(nc.psum_tensor(f"pU{i}", [128, 512], F32)) for i in range(2)]
        pDA = En(nc.psum_tensor("pDA", [128, 1024], F32))
        pDB = En(nc.psum_tensor("pDB", [128, 1024], F32))
        pW = En(nc.psum_tensor("pW", [128, 512], F32))
        pT = En(nc.psum_tensor("pT", [128, 128], BF16))
        pD = [pDA, pDB]

        b = Build(nc)
        for s in ("xd", "ld", "wd0", "wd1", "stg", "st", "cc", "ccd", "pe", "ve", "se"):
            b.sem(s)

        xr3 = x_h.ap().rearrange("(i p) d -> p i d", p=128)
        step = max(1, NT // 4)
        for j in range(0, NT, step):
            hi = min(NT, j + step)
            b.op("sync", "dma_start",
                 out=xres[:, :].rearrange("p (i d) -> p i d", d=D)[:, j:hi, :],
                 in_=xr3[:, j:hi, :], inc=("xd", 16))
        for dst, src in ((ident_bs, ident_b), (ident_fs, ident_f), (ut_fs, ut_f),
                         (ut_bs, ut_b), (ones65_s, ones65), (onescol_s, onescol),
                         (sel_s, sel_h), (rkm_s, rkm_h)):
            b.op("sync", "dma_start", out=dst[:, :], in_=src.ap(), inc=("xd", 16))
        XD_ALL = b.cnt["xd"]

        b.op("gpsimd", "dma_start",
             out=wg_s[:, :].rearrange("p (k g) -> p k g", g=E),
             in_=wg_h.ap().rearrange("(k p) g -> p k g", p=128), inc=("ld", 16))
        for (dst, src) in ((wq_s, wq_h), (wk_s, wk_h), (wv_s, wv_h)):
            b.op("gpsimd", "dma_start",
                 out=dst[:, :].rearrange("p (k h) -> p k h", h=HD),
                 in_=src.ap().rearrange("(k p) h -> p k h", p=128), inc=("ld", 16))
        b.op("gpsimd", "dma_start", out=wo_s[:, :], in_=wo_h.ap(), inc=("ld", 16))
        b.op("gpsimd", "dma_start", out=bdn_s[:, :], in_=bdn_h.ap(), inc=("ld", 16))
        LD_ALL = b.cnt["ld"]

        TR = {"evac": 0, "xn_free": {0: 0, 1: 0}, "ln_ev": {0: 0, 1: 0}}
        LNST = {"std": 0}
        GST = {"exp": {0: 0}, "tr": {0: 0}}
        GL = {"ev": 0}

        def gate_logits_cb(i, tile_ve):
            gwL = scr[:, 0:NT * E]
            if i == 0:
                b.wait("tensor", "ld", LD_ALL)
                b.wait("tensor", "ve", CWB["evac"])
            b.wait("tensor", "ve", GL["ev"])
            b.wait("tensor", "ve", tile_ve)
            mms = [(pW[:, 0:E], xnt[:, k * TC + i * 128: k * TC + (i + 1) * 128],
                    wg_s[:, k * E:(k + 1) * E], k == 0, k == DCH - 1) for k in range(DCH)]
            vmm = b.mmgroup(mms, inc=("pe", 1))
            b.wait("vector", "pe", vmm)
            if i == 0:
                b.wait("vector", "se", GST["exp"][0])
            GL["ev"] = b.op("vector", "tensor_copy", gwL[:, i * E:(i + 1) * E],
                            pW[0:128, 0:E], inc=("ve", 1))
        CWB = {"evac": 0}
        W = {"gq": 0, "qres": [], "bup_free": {0: 0, 1: 0}, "bup_v": 0}

        def gen_ln(sq_wait, xnt_busy_pe, scr_free_ve=0, pu_free_se=0, pu_free_ve=0, tile_cb=None):
            NBN = (D + 511) // 512
            b.wait("vector", "se", LNST["std"])
            for i in range(NT):
                if i == 0:
                    b.wait("vector", "xd", XD_ALL)
                sb = (i % 2) * 32
                xt = xres[:, i * D:(i + 1) * D]
                b.wait("vector", "ve", sq_wait[i])
                for j in range(NBN):
                    w0 = j * 512
                    w1 = min(D, w0 + 512)
                    b.op("vector", "bn_stats", stw[:, sb + 16 + 6 * j: sb + 22 + 6 * j],
                         xt[:, w0:w1])
                b.op("vector", "bn_aggr", lnmv[:, 2 * i:2 * i + 2],
                     stw[:, sb + 16: sb + 16 + 6 * NBN])
            lvar = lnmv[:, :].rearrange("p (i two) -> p i two", two=2)[:, :, 1:2]
            vvar = b.op("vector", "tensor_scalar_add", lvar, lvar, 1e-5, inc=("ve", 1))
            b.wait("scalar", "ve", vvar)
            vstd = b.op("scalar", "activation", lvar, lvar, AF.Sqrt, inc=("se", 1))
            LNST["std"] = vstd
            b.wait("vector", "se", vstd)
            b.op("vector", "reciprocal", lnr[:, 0:NT], lvar)
            for i in range(NT):
                xt = xres[:, i * D:(i + 1) * D]
                b.wait("vector", "pe", TR["xn_free"][0])
                vno = b.op("vector", "tensor_scalar", xn[:, 0:D], xt,
                           lnmv[:, 2 * i:2 * i + 1], lnr[:, i:i + 1],
                           ALU.subtract, ALU.mult, inc=("ve", 1))
                b.wait("tensor", "ve", vno)
                if i == 0:
                    b.wait("tensor", "xd", XD_ALL)
                    b.wait("tensor", "se", pu_free_se)
                    b.wait("tensor", "ve", pu_free_ve)
                for k in range(DCH):
                    bk2 = k % 2
                    b.wait("tensor", "ve", TR["ln_ev"][bk2])
                    vtr = b.op("tensor", "transpose", pU[bk2][:, 0:128],
                               xn[:, k * 128:(k + 1) * 128],
                               ident_fs[:, :], inc=("pe", 1))
                    if k == DCH - 1:
                        TR["xn_free"][0] = vtr
                    b.wait("vector", "pe", vtr)
                    if i == 0 and k == 0:
                        b.wait("vector", "pe", xnt_busy_pe)
                    TR["ln_ev"][bk2] = b.op("vector", "tensor_copy",
                                            xnt[:, k * TC + i * 128: k * TC + (i + 1) * 128],
                                            pU[bk2][0:128, 0:128], inc=("ve", 1))
                TR["evac"] = max(TR["ln_ev"][0], TR["ln_ev"][1])
                if tile_cb is not None:
                    tile_cb(i, max(TR["ln_ev"][0], TR["ln_ev"][1]))
            return max(TR["ln_ev"][0], TR["ln_ev"][1])

        def gen_gate(xnt_ve):
            GE = NT * E
            gwL = scr[:, 0:GE]
            gwB = scr[:, GE:2 * GE]
            m1 = gsc[:, 0:NT]
            m2r = gsc[:, 16:16 + NT]
            m3r = gsc[:, 32:32 + NT]
            rsg = gsc[:, 48:48 + NT]

            def seg(ap):
                return ap.rearrange("p (g e) -> p g e", e=E)

            def bc(ap_col):
                base = ap_col
                return bass.AP(base.tensor, base.offset,
                               [[base.ap[0][0], 128], [1, NT], [0, E]])

            b.op("vector", "tensor_reduce", m1[:, :].rearrange("p (g o) -> p g o", o=1),
                 seg(gwL), axis=AX.X, op=ALU.max)
            b.op("vector", "tensor_tensor", seg(gwC[:, :]), seg(gwL), bc(m1), ALU.is_ge)
            b.op("vector", "scalar_tensor_tensor", gwB, gwC[:, :], -1e30, gwL[:, 0:GE],
                 ALU.mult, ALU.add)
            b.op("vector", "tensor_reduce", m2r[:, :].rearrange("p (g o) -> p g o", o=1),
                 seg(gwB), axis=AX.X, op=ALU.max)
            b.op("vector", "tensor_tensor", seg(gwC[:, :]), seg(gwB), bc(m2r), ALU.is_ge)
            b.op("vector", "scalar_tensor_tensor", gwB, gwC[:, :], -1e30, gwB,
                 ALU.mult, ALU.add)
            b.op("vector", "tensor_reduce", m3r[:, :].rearrange("p (g o) -> p g o", o=1),
                 seg(gwB), axis=AX.X, op=ALU.max)
            vmask = b.op("vector", "tensor_tensor", seg(gwC[:, :]), seg(gwL), bc(m3r),
                         ALU.is_ge, inc=("ve", 1))
            vlc = b.op("vector", "tensor_tensor", seg(gwL), seg(gwL), bc(m1), ALU.subtract)
            b.wait("scalar", "ve", vlc)
            vexp = b.op("scalar", "activation", gwB, gwL[:, 0:GE], AF.Exp, inc=("se", 1))
            GST["exp"][0] = vexp
            b.wait("vector", "se", vexp)
            b.op("vector", "tensor_mul", gwB, gwB, gwC[:, :])
            b.op("vector", "tensor_reduce", rsg[:, :].rearrange("p (g o) -> p g o", o=1),
                 seg(gwB), axis=AX.X, op=ALU.add)
            b.op("vector", "reciprocal", rsg[:, :], rsg[:, :])
            b.op("vector", "tensor_tensor", seg(gwB), seg(gwB), bc(rsg), ALU.mult)
            b.wait("vector", "pe", GST["tr"][0])
            vcw = b.op("vector", "tensor_scalar_mul", cwt[:, 0:GE], gwB, 0.5, inc=("ve", 1))
            prev_evac = 0
            ev = [0, 0]
            for i in range(NT):
                bk = i % 2
                b.wait("tensor", "ve", vcw if i == 0 else 0)
                if ev[bk]:
                    b.wait("tensor", "ve", ev[bk])
                vtr = b.op("tensor", "transpose", pU[bk][0:E, 0:128],
                           cwt[:, i * E:(i + 1) * E], ident_fs[:, :], inc=("pe", 1))
                GST["tr"][0] = vtr
                b.wait("vector", "pe", vtr)
                ev[bk] = b.op("vector", "tensor_copy", cwT[0:E, i * 128:(i + 1) * 128],
                              pU[bk][0:E, 0:128], inc=("ve", 1))
                prev_evac = ev[bk]
            return max(ev[0], ev[1])

        def emit_wdma(e, q):
            gq = W["gq"]
            par = gq % 2
            if q == 0:
                ep = (W["gq"] // NQ) % 2
                if W["bup_free"][ep]:
                    b.wait("scalar", "se", W["bup_free"][ep])
                b.op("scalar", "dma_start", out=bupT_s[:, ep * HC_ALL:(ep + 1) * HC_ALL],
                     in_=bupT_h.ap()[e], inc=("stg", 16))
                W["bup_v"] = b.cnt["stg"]
            if gq >= 2:
                b.wait("gpsimd", "ve", W["qres"][gq - 2])
            wdsem = f"wd{par}"
            b.op("gpsimd", "dma_start",
                 out=wup_b[par][:, :].rearrange("p (k h) -> p k h", h=QH),
                 in_=wup_h.ap()[e, :, q * QH:(q + 1) * QH].rearrange("(k p) h -> p k h", p=128),
                 inc=(wdsem, 16))
            b.op("gpsimd", "dma_start",
                 out=wdn_b[par][:, :].rearrange("p (c d) -> p c d", d=D),
                 in_=wdn_h.ap()[e, q * QH:(q + 1) * QH, :].rearrange("(c p) d -> p c d", p=128),
                 inc=(wdsem, 16))
            W["gq"] += 1
            return gq

        def gen_ffn(xnt_ve, cw_ve, scr_free_ve, pre_waits):
            gelu_hist, cwm_hist, res_hist = [], [], []
            slice_res = [0] * NT
            dn_cnt = 0
            first = True
            dnlast = 0
            for e in range(E):
                for q in range(NQ):
                    gq = emit_wdma(e, q)
                    par = gq % 2
                    b.wait("tensor", f"wd{gq % 2}", 32 * (gq // 2 + 1))
                    if first:
                        b.wait("tensor", "ve", cw_ve)
                        b.wait("tensor", "ve", xnt_ve)
                        for (s_, v_) in pre_waits:
                            b.wait("tensor", s_, v_)
                        first = False
                    for tg in range(NTG):
                        b.wait("tensor", "ve", CWB["evac"])
                        vcwb = b.op("tensor", "matmul", pW[:, 0:TGW],
                                    sel_s[:, e * 128:(e + 1) * 128],
                                    cwT[0:E, tg * TGW:(tg + 1) * TGW],
                                    start=True, stop=True, inc=("pe", 1))
                        b.wait("vector", "pe", vcwb)
                        CWB["evac"] = b.op("vector", "tensor_copy", cwB[:, 0:TGW], pW[:, 0:TGW],
                                           inc=("ve", 1))
                        for hc in range(QHC):
                            g = len(gelu_hist)
                            bk = g % 2
                            if g >= 2:
                                b.wait("tensor", "se", gelu_hist[-2])
                            mms = [(pU[bk][:, 0:TGW],
                                    wup_b[par][:, k * QH + hc * 128: k * QH + (hc + 1) * 128],
                                    xnt[:, k * TC + tg * TGW: k * TC + (tg + 1) * TGW],
                                    k == 0, k == DCH - 1) for k in range(DCH)]
                            vup = b.mmgroup(mms, inc=("pe", 1))
                            b.wait("scalar", "pe", vup)
                            b.wait("scalar", "ve",
                                   cwm_hist[-2] if len(cwm_hist) >= 2 else scr_free_ve)
                            if q == 0 and tg == 0 and hc == 0:
                                b.wait("scalar", "stg", W["bup_v"])
                            ep = e % 2
                            bias = bupT_s[:, ep * HC_ALL + q * QHC + hc: ep * HC_ALL + q * QHC + hc + 1]
                            vg = b.op("scalar", "activation", scrb[:, bk * TGW:(bk + 1) * TGW],
                                      pU[bk][:, 0:TGW], AF.Gelu, bias=bias, inc=("se", 1))
                            gelu_hist.append(vg)
                            if q == NQ - 1 and tg == NTG - 1 and hc == QHC - 1:
                                W["bup_free"][ep] = vg
                            b.wait("vector", "se", vg)
                            b.wait("vector", "pe", dnlast)
                            vcm = b.op("vector", "tensor_mul", hT[:, hc * TGW:(hc + 1) * TGW],
                                       scrb[:, bk * TGW:(bk + 1) * TGW], cwB[:, 0:TGW],
                                       inc=("ve", 1))
                            cwm_hist.append(vcm)
                        tg_cwm = cwm_hist[-QHC:]
                        for sl in range(NSL):
                            tile = tg * NSL + sl
                            bk = dn_cnt % 2
                            mms = []
                            if e == 0 and q == 0:
                                for nh in range(NNH):
                                    mms.append((pD[bk][:, nh * NW:(nh + 1) * NW],
                                                cwT[0:E, tile * 128:(tile + 1) * 128],
                                                bdn_s[0:E, nh * NW:(nh + 1) * NW], True, False))
                            for c in range(QHC):
                                if sl == 0:
                                    mms.append(("wait", "ve", tg_cwm[c]))
                                st_ = (c == 0) and not (e == 0 and q == 0)
                                for nh in range(NNH):
                                    mms.append((pD[bk][:, nh * NW:(nh + 1) * NW],
                                                hT[:, c * TGW + sl * 128: c * TGW + (sl + 1) * 128],
                                                wdn_b[par][:, c * D + nh * NW: c * D + (nh + 1) * NW],
                                                st_, c == QHC - 1))
                            if sl > 0:
                                b.wait("tensor", "ve", tg_cwm[-1])
                            if len(res_hist) >= 2:
                                b.wait("tensor", "ve", res_hist[-2])
                            vdn = b.mmgroup(mms, inc=("pe", 1))
                            dnlast = vdn
                            dn_cnt += 1
                            b.wait("vector", "pe", vdn)
                            xt = xres[:, tile * D:(tile + 1) * D]
                            vres = b.op("vector", "tensor_add", xt, pD[bk][:, 0:D], xt, inc=("ve", 1))
                            res_hist.append(vres)
                            slice_res[tile] = vres
                    W["qres"].append(res_hist[-1])
            return slice_res, gelu_hist[-1], cwm_hist[-1]

        ln1_ve = gen_ln([0] * NT, 0, tile_cb=gate_logits_cb)
        cw1_ve = gen_gate(ln1_ve)
        CWB["evac"] = max(CWB["evac"], GL["ev"])
        res1, gelu1_se, cwm1_ve = gen_ffn(ln1_ve, cw1_ve, 0, [])
        final_res = res1

        if stage >= 2:
            pe_f1 = b.cnt["pe"]
            maxres1 = max(res1)
            ln2_ve = gen_ln(res1, pe_f1, cwm1_ve, pu_free_se=gelu1_se)

            b.op("vector", "memset", carr[:, 0:HD], 0.0)
            pack_ve = 0
            vn_ev = 0
            kn_ve = [0] * NT
            knf = {0: 0, 1: 0}
            vnf = {0: maxres1, 1: maxres1}
            vn_evs = [0] * NT
            for c in range(NT):
                bk = c % 2
                if c == 0:
                    b.wait("tensor", "ve", ln2_ve)
                    b.wait("tensor", "se", gelu1_se)
                b.wait("tensor", "ve", knf[bk])
                mms = [(pU[bk][:, 0:HD], xnt[:, k * TC + c * 128: k * TC + (c + 1) * 128],
                        wk_s[:, k * HD:(k + 1) * HD], k == 0, k == DCH - 1) for k in range(DCH)]
                vk = b.mmgroup(mms, inc=("pe", 1))
                b.wait("tensor", "ve", vnf[bk])
                mms = [(pD[bk][:, 0:HD], xnt[:, k * TC + c * 128: k * TC + (c + 1) * 128],
                        wv_s[:, k * HD:(k + 1) * HD], k == 0, k == DCH - 1) for k in range(DCH)]
                vv = b.mmgroup(mms, inc=("pe", 1))
                b.wait("vector", "pe", vk)
                b.op("vector", "tensor_scalar_min", scr[:, 0:HD], pU[bk][:, 0:HD], 0.0)
                vmx = b.op("vector", "tensor_scalar_max", scr[:, HD:2 * HD], pU[bk][:, 0:HD], 0.0,
                           inc=("ve", 1))
                knf[bk] = vmx
                b.wait("scalar", "ve", vmx)
                vex = b.op("scalar", "activation", scr[:, 0:HD], scr[:, 0:HD], AF.Exp, inc=("se", 1))
                b.wait("vector", "se", vex)
                kn_ve[c] = b.op("vector", "tensor_add", kn[:, c * HD:(c + 1) * HD],
                                scr[:, 0:HD], scr[:, HD:2 * HD], inc=("ve", 1))
                b.wait("vector", "pe", vv)
                vn_ev = b.op("vector", "tensor_copy", vn[:, bk * HD:(bk + 1) * HD],
                             pD[bk][:, 0:HD], inc=("ve", 1))
                vnf[bk] = vn_ev
                vn_evs[c] = vn_ev
                b.wait("tensor", "ve", vn_ev)
                b.wait("tensor", "ve", pack_ve if pack_ve else CWB["evac"])
                mms = [(pW[0:HD, 0:HD], vn[:, bk * HD:(bk + 1) * HD],
                        kn[:, c * HD:(c + 1) * HD], True, True),
                       (pW[64:65, 0:HD], onescol_s[:, :], kn[:, c * HD:(c + 1) * HD], True, True)]
                vpk = b.mmgroup(mms, inc=("pe", 1))
                b.wait("vector", "pe", vpk)
                pack_ve = b.op("vector", "tensor_add", carr[:, (c + 1) * HD:(c + 2) * HD],
                               pW[0:65, 0:HD], carr[:, c * HD:(c + 1) * HD], inc=("ve", 1))
            CWB["evac"] = pack_ve

            b.wait("scalar", "ve", pack_ve)
            b.op("scalar", "dma_start", out=cc_in.ap(), in_=carr[:, NT * HD:(NT + 1) * HD],
                 inc=("stg", 16))
            b.wait("gpsimd", "stg", b.cnt["stg"])
            vcc = b.op("gpsimd", "collective_compute", "AllGather", ALU.bypass,
                       replica_groups=[list(range(cores))],
                       ins=[cc_in.ap().opt()], outs=[cc_out.ap().opt()], inc=("cc", 1))
            b.wait("gpsimd", "cc", vcc)
            b.wait("gpsimd", "ve", pack_ve)
            b.op("gpsimd", "dma_start",
                 out=scr[0:65, 0:cores * HD].rearrange("p (r h) -> p r h", h=HD),
                 in_=cc_out.ap().rearrange("(r p) h -> p r h", p=65), inc=("ccd", 16))
            b.wait("vector", "ccd", b.cnt["ccd"])
            b.op("vector", "memset", pin[:, :], 0.0)
            for r in range(cores):
                b.op("vector", "scalar_tensor_tensor", pin[0:65, :],
                     scr[0:65, r * HD:(r + 1) * HD], rkm_s[:, r:r + 1], pin[0:65, :],
                     ALU.mult, ALU.add)
            pin_b = bass.AP(pin[:, :].tensor, pin[:, :].offset,
                            [[pin[:, :].ap[0][0], 65], [0, NT], [1, HD]])
            vcb_all = b.op("vector", "tensor_tensor",
                           carrb[:, 0:NT * HD].rearrange("p (c h) -> p c h", h=HD),
                           carr[:, 0:NT * HD].rearrange("p (c h) -> p c h", h=HD),
                           pin_b, ALU.add, inc=("ve", 1))

            o_se = 0
            den_ve = 0
            ao_res = 0
            res2 = [0] * NT
            vmsk_arr = [0] * NT
            prev_vtc = 0
            prev_vmsk = 0
            otr = {0: 0, 1: 0}
            attn_scr_last = 0

            def stageA(c):
                nonlocal prev_vtc, prev_vmsk, attn_scr_last
                p2 = c % 2
                b.wait("tensor", "ve", prev_vtc if prev_vtc else vn_ev)
                mms = [(pU[0][0:HD, 0:128], wq_s[:, k * HD:(k + 1) * HD],
                        xnt[:, k * TC + c * 128: k * TC + (c + 1) * 128],
                        k == 0, k == DCH - 1) for k in range(DCH)]
                vqt = b.mmgroup(mms, inc=("pe", 1))
                b.wait("vector", "pe", vqt)
                b.op("vector", "tensor_scalar_min", scr[0:HD, 0:128], pU[0][0:HD, 0:128], 0.0)
                vmx = b.op("vector", "tensor_scalar_max", scr[0:HD, 128:256],
                           pU[0][0:HD, 0:128], 0.0, inc=("ve", 1))
                b.wait("scalar", "ve", vmx)
                vex = b.op("scalar", "activation", scr[0:HD, 0:128], scr[0:HD, 0:128],
                           AF.Exp, inc=("se", 1))
                b.wait("vector", "se", vex)
                vq_add = b.op("vector", "tensor_add", qTc[0:HD, p2 * 128:(p2 + 1) * 128],
                              scr[0:HD, 0:128], scr[0:HD, 128:256], inc=("ve", 1))
                b.wait("tensor", "ve", vq_add)
                b.wait("tensor", "ve", TR["evac"])
                vtrq = b.op("tensor", "transpose", pT[:, 0:HD],
                            qTc[0:HD, p2 * 128:(p2 + 1) * 128], ident_bs[0:HD, 0:HD],
                            inc=("pe", 1))
                b.wait("vector", "pe", vtrq)
                TR["evac"] = b.op("vector", "tensor_copy", qn[:, p2 * HD:(p2 + 1) * HD],
                                  pT[:, 0:HD], inc=("ve", 1))
                attn_scr_last = vq_add
                b.wait("tensor", "ve", vq_add)
                mms = [(pU[0][0:HD, 0:128], wv_s[:, k * HD:(k + 1) * HD],
                        xnt[:, k * TC + c * 128: k * TC + (c + 1) * 128],
                        k == 0, k == DCH - 1) for k in range(DCH)]
                vvt = b.mmgroup(mms, inc=("pe", 1))
                b.wait("vector", "pe", vvt)
                prev_vtc = b.op("vector", "tensor_copy", vTc[0:HD, p2 * 128:(p2 + 1) * 128],
                                pU[0][0:HD, 0:128], inc=("ve", 1))
                b.wait("tensor", "ve", prev_vtc)
                vB = b.op("tensor", "matmul", pU[1][:, 0:128],
                          vTc[0:HD, p2 * 128:(p2 + 1) * 128],
                          qTc[0:HD, p2 * 128:(p2 + 1) * 128],
                          start=True, stop=True, inc=("pe", 1))
                b.wait("vector", "pe", vB)
                prev_vmsk = b.op("vector", "tensor_mul", Bm[:, p2 * 128:(p2 + 1) * 128],
                                 pU[1][:, 0:128], ut_fs[:, :], inc=("ve", 1))
                vmsk_arr[c] = prev_vmsk

            def stageB(c):
                nonlocal o_se, den_ve, ao_res
                p2 = c % 2
                b.wait("tensor", "ve", vcb_all)
                b.wait("tensor", "se", o_se)
                if c < 2:
                    b.wait("tensor", "ve", max(maxres1, vn_evs[NT - 2 + c]))
                mms = [(pDA[:, 0:HD], Bm[:, p2 * 128:(p2 + 1) * 128],
                        kn[:, c * HD:(c + 1) * HD], True, False),
                       (pDA[:, 0:HD], qTc[0:HD, p2 * 128:(p2 + 1) * 128],
                        carrb[0:HD, c * HD:(c + 1) * HD], False, True)]
                vO = b.mmgroup(mms, inc=("pe", 1))
                b.wait("tensor", "ve", den_ve)
                mms = [(pDA[:, 512:512 + HD], ut_bs[:, :], kn[:, c * HD:(c + 1) * HD], True, False),
                       (pDA[:, 512:512 + HD], ones65_s[64:65, 0:128],
                        carrb[64:65, c * HD:(c + 1) * HD], False, True)]
                vZ = b.mmgroup(mms, inc=("pe", 1))
                b.wait("vector", "pe", vZ)
                b.wait("vector", "se", o_se)
                b.op("vector", "tensor_mul", tqz[:, 0:HD], pDA[:, 512:512 + HD],
                     qn[:, p2 * HD:(p2 + 1) * HD])
                b.op("vector", "reduce_sum", stw[:, 14:15], tqz[:, 0:HD], axis=AX.X)
                b.op("vector", "tensor_scalar_max", stw[:, 14:15], stw[:, 14:15], 1e-6)
                den_ve = b.op("vector", "reciprocal", stw[:, 15:16], stw[:, 14:15], inc=("ve", 1))
                b.wait("scalar", "ve", den_ve)
                b.wait("scalar", "pe", vO)
                b.wait("scalar", "pe", otr[p2])
                o_se = b.op("scalar", "activation", osb[:, p2 * HD:(p2 + 1) * HD], pDA[:, 0:HD],
                            AF.Copy, scale=stw[:, 15:16], inc=("se", 1))
                b.wait("tensor", "se", o_se)
                b.wait("tensor", "ve", TR["evac"])
                vtr = b.op("tensor", "transpose", pT[0:HD, 0:128], osb[:, p2 * HD:(p2 + 1) * HD],
                           ident_bs[:, :], inc=("pe", 1))
                otr[p2] = vtr
                b.wait("vector", "pe", vtr)
                TR["evac"] = b.op("vector", "tensor_copy", oT[0:HD, p2 * 128:(p2 + 1) * 128],
                                  pT[0:HD, 0:128], inc=("ve", 1))
                b.wait("tensor", "ve", TR["evac"])
                b.wait("tensor", "ve", ao_res if ao_res else max(maxres1, vn_evs[NT - 1]))
                mms = [(pDB[:, nh * NW:(nh + 1) * NW], oT[0:HD, p2 * 128:(p2 + 1) * 128],
                        wo_s[0:HD, nh * NW:(nh + 1) * NW], True, True) for nh in range(NNH)]
                vAO = b.mmgroup(mms, inc=("pe", 1))
                b.wait("vector", "pe", vAO)
                xt = xres[:, c * D:(c + 1) * D]
                ao_res = b.op("vector", "tensor_add", xt, pDB[:, 0:D], xt, inc=("ve", 1))
                res2[c] = ao_res

            stageA(0)
            for c in range(NT):
                if c + 1 < NT:
                    stageA(c + 1)
                stageB(c)
            final_res = res2


        if stage >= 3:
            CWB["evac"] = max(CWB["evac"], pack_ve)
            ln3_ve = gen_ln(res2, b.cnt["pe"], attn_scr_last, pu_free_ve=max(res2),
                            tile_cb=gate_logits_cb)
            cw3_ve = gen_gate(ln3_ve)
            pre = [("se", o_se), ("ve", ao_res), ("ve", den_ve)]
            CWB["evac"] = max(CWB["evac"], GL["ev"])
            res3, _, _ = gen_ffn(ln3_ve, cw3_ve, attn_scr_last, pre)
            final_res = res3

        b.wait("sync", "ve", max(final_res))
        yr3 = y_h.ap().rearrange("(i p) d -> p i d", p=128)
        for j in range(0, NT, step):
            hi = min(NT, j + step)
            b.op("sync", "dma_start",
                 out=yr3[:, j:hi, :],
                 in_=xres[:, :].rearrange("p (i d) -> p i d", d=D)[:, j:hi, :], inc=("st", 16))
        b.wait("sync", "st", b.cnt["st"])

        import os as _os
        if _os.environ.get("KBUILD_DEBUG"):
            print("SBUF remaining:", nc.sbuf_bytes_remaining, "PSUM banks left:", nc.psum_banks_remaining)
        with nc.Block() as block:
            @block.tensor
            def _(t):
                for f in b.P["tensor"]:
                    f(t)

            @block.vector
            def _(v):
                for f in b.P["vector"]:
                    f(v)

            @block.scalar
            def _(s):
                for f in b.P["scalar"]:
                    f(s)

            @block.gpsimd
            def _(g):
                for f in b.P["gpsimd"]:
                    f(g)

            @block.sync
            def _(sy):
                for f in b.P["sync"]:
                    f(sy)

    return nc


_build_cache = {}


def get_nc(cfg=None, n_cores=None, stage=3):
    key = (tuple(sorted((cfg or REAL_CFG).items())), n_cores, stage)
    if key not in _build_cache:
        _build_cache[key] = build_nc(dict(cfg or REAL_CFG), n_cores, stage)
    return _build_cache[key]


def make_in_maps(inputs, cfg=None, n_cores=8):
    cfg = dict(cfg or REAL_CFG)
    TC, D, E, HID = cfg["TC"], cfg["D"], cfg["E"], cfg["HID"]
    x = np.ascontiguousarray(np.asarray(inputs["L0_attn_x"], dtype=np.float32).reshape(-1, D))
    bup = np.asarray(inputs["L0_ffn_bup"], dtype=np.float32)
    bupT = np.ascontiguousarray(bup.reshape(E, HID // 128, 128).transpose(0, 2, 1))
    common = dict(
        wq=np.ascontiguousarray(np.asarray(inputs["L0_attn_W_q"], dtype=np.float32)),
        wk=np.ascontiguousarray(np.asarray(inputs["L0_attn_W_k"], dtype=np.float32)),
        wv=np.ascontiguousarray(np.asarray(inputs["L0_attn_W_v"], dtype=np.float32)),
        wo=np.ascontiguousarray(np.asarray(inputs["L0_attn_W_o"], dtype=np.float32)),
        wg=np.ascontiguousarray(np.asarray(inputs["L0_ffn_Wg"], dtype=np.float32)),
        wup=np.ascontiguousarray(np.asarray(inputs["L0_ffn_Wup"], dtype=np.float32)),
        bupT=bupT,
        wdn=np.ascontiguousarray(np.asarray(inputs["L0_ffn_Wdown"], dtype=np.float32)),
        bdn=np.ascontiguousarray(np.asarray(inputs["L0_ffn_bdown"], dtype=np.float32)),
    )
    in_maps = []
    for c in range(n_cores):
        rkm = np.zeros((65, 8), dtype=np.float32)
        rkm[:, :c] = 1.0
        m = dict(common)
        m["x"] = np.ascontiguousarray(x[c * TC:(c + 1) * TC])
        m["rkm"] = rkm
        in_maps.append(m)
    return in_maps


def kernel(**inputs):
    nc = get_nc()
    in_maps = make_in_maps(inputs)
    res = run_bass_kernel_spmd(nc, in_maps, core_ids=list(range(8)))
    out = np.concatenate([r["y"] for r in res.results], axis=0)
    return out.reshape(1, 16384, 1024)
